# version 15
# baseline (speedup 1.0000x reference)
import math
from collections import defaultdict
from contextlib import ExitStack

import numpy as np

import concourse.bass as bass
from concourse.bass_core_rust import IndirectOffsetOnAxis
import concourse.mybir as mybir
from concourse.bass_utils import run_bass_kernel_spmd

F32 = mybir.dt.float32
BF16 = mybir.dt.bfloat16
AF = mybir.ActivationFunctionType
ALU = mybir.AluOpType
AX = mybir.AxisListType.X
I32 = mybir.dt.int32

D = 2048
S = 4096
NB = 4
NHEADS = 16
NH = 8
HD = 128
NE = 8
NDC = 16
NSUP = 8
TOWN = 2048
TPASS = 1024
NPASS = 2
ALPHA = 1.0 / math.sqrt(HD)
NEG = -1e38
EPS = 1e-5

ENGS = ("PE", "ACT", "DVE", "POOL", "SP")
DBG = False


class Prog:

    def __init__(self):
        self.ops = {e: [] for e in ENGS}
        self.cnt = defaultdict(int)
        self.bar_val = 0

    def add(self, eng, fn, waits=(), inc=None, incn=1):
        w = [(s, v) for (s, v) in waits if v > 0]
        if self.bar_val > 0:
            w.append(("sBAR", self.bar_val))
        self.ops[eng].append((fn, w, inc, incn))
        if inc is not None:
            self.cnt[inc] += incn
            return self.cnt[inc]
        return None

    def barrier(self):
        save = self.bar_val
        self.bar_val = 0
        for e in ENGS:
            self.add(e, lambda x: x.nop(), waits=[("sBAR", save)],
                     inc="sBAR")
        self.bar_val = self.cnt["sBAR"]

    def emit(self, block, sems):
        section_of = {
            "PE": block.tensor, "ACT": block.scalar, "DVE": block.vector,
            "POOL": block.gpsimd, "SP": block.sync,
        }
        for eng_name in ENGS:
            ops_local = self.ops[eng_name]
            if not ops_local:
                continue

            def make(ops_l):
                def body(eng):
                    known = defaultdict(int)
                    for fn, waits, inc, incn in ops_l:
                        for semname, val in waits:
                            if val > known[semname]:
                                eng.wait_ge(sems[semname], val)
                                known[semname] = val
                        instr = fn(eng)
                        if inc is not None:
                            instr.then_inc(sems[inc], incn)
                return body
            section_of[eng_name](make(ops_local))


def build_program():
    nc = bass.Bass("TRN2", target_bir_lowering=False)

    xin = nc.dram_tensor("xin", [S, D], F32, kind="ExternalInput").ap()
    xown = nc.dram_tensor("xown", [TOWN, D], F32, kind="ExternalInput").ap()
    tri_d = nc.dram_tensor("trid", [128, 128], F32, kind="ExternalInput").ap()
    iot_d = nc.dram_tensor("iotd", [128, 16], F32, kind="ExternalInput").ap()
    io0_d = nc.dram_tensor("io0d", [128, 1], mybir.dt.int32, kind="ExternalInput").ap()
    bsr_d = nc.dram_tensor("bsrd", [1, D], F32, kind="ExternalInput").ap()
    A1 = nc.dram_tensor("A1", [NH, 128, NDC * 128], F32, kind="ExternalInput").ap()
    KV1 = nc.dram_tensor("KV1", [128, NDC * 256], F32, kind="ExternalInput").ap()
    R1 = nc.dram_tensor("R1", [128, NDC * 8], F32, kind="ExternalInput").ap()
    P5 = nc.dram_tensor("P5", [NDC, 128, 2048], F32, kind="ExternalInput").ap()
    W1 = nc.dram_tensor("W1", [NE * 32, 128, NDC * 256], F32, kind="ExternalInput").ap()
    M4 = nc.dram_tensor("M4", [NE * 32, 128, 2048], F32, kind="ExternalInput").ap()
    bq_d = nc.dram_tensor("bqd", [128, NH], F32, kind="ExternalInput").ap()
    bk_d = nc.dram_tensor("bkd", [128, 1], F32, kind="ExternalInput").ap()
    bv_d = nc.dram_tensor("bvd", [1, 128], F32, kind="ExternalInput").ap()
    bs_d = nc.dram_tensor("bsd", [128, NDC], F32, kind="ExternalInput").ap()
    msk_d = nc.dram_tensor("mskd", [128, 2048], F32, kind="ExternalInput").ap()
    idn_d = nc.dram_tensor("idnd", [128, 128], F32, kind="ExternalInput").ap()
    hm_d = nc.dram_tensor("hmd", [128, 2], F32, kind="ExternalInput").ap()
    y = nc.dram_tensor("y", [TOWN, D], F32, kind="ExternalOutput").ap()
    dbg = {}
    if DBG:
        dbg["k"] = nc.dram_tensor("dbg_k", [128, S], BF16, kind="ExternalOutput").ap()
        dbg["v"] = nc.dram_tensor("dbg_v", [128, S], BF16, kind="ExternalOutput").ap()
        dbg["q"] = nc.dram_tensor("dbg_q", [128, S], BF16, kind="ExternalOutput").ap()
        dbg["cco"] = nc.dram_tensor("dbg_cco", [NHEADS * 128, TOWN], BF16, kind="ExternalOutput").ap()
        dbg["gates"] = nc.dram_tensor("dbg_gates", [128, 8 * NE], F32, kind="ExternalOutput").ap()
        dbg["xnp"] = nc.dram_tensor("dbg_xnp", [128, NDC * TPASS], BF16, kind="ExternalOutput").ap()
        dbg["acc"] = nc.dram_tensor("dbg_acc", [128, NDC * TPASS], F32, kind="ExternalOutput").ap()
    CAPD = 768
    xe_d = nc.dram_tensor("xe_dr", [NE * CAPD + 1, D], BF16, kind="Internal").ap()
    o_d = nc.dram_tensor("o_dr", [NE * CAPD + 1, D], F32, kind="Internal").ap()
    acc_d = nc.dram_tensor("acc_dr", [TOWN, D], F32, kind="Internal").ap()
    cc_in = nc.dram_tensor("cc_in", [2 * NHEADS * 128, TOWN], BF16, kind="Internal").ap()
    cc_out = nc.dram_tensor("cc_out", [NHEADS * 128, TOWN], BF16, kind="Internal").ap()

    P = Prog()
    C = P.cnt

    def dma(out, in_):
        return lambda e: e.dma_start(out=out, in_=in_)

    with ExitStack() as top:
        def sbuf(name, shape, dt=F32):
            return top.enter_context(nc.sbuf_tensor(name, shape, dt))

        sems = {}
        for sname in ("dSP", "dWQ", "dOUT", "sPE", "sACT", "sDVE", "sPOOL", "sBAR"):
            sems[sname] = top.enter_context(nc.semaphore(sname))

        ident_b = sbuf("ident_b", [128, 128], BF16)
        ident_f = sbuf("ident_f", [128, 128], F32)
        mask = sbuf("mask", [128, 2048], F32)
        onesr = sbuf("onesr", [1, 128], F32)
        bq = sbuf("bq_s", [128, NH], F32)
        bk = sbuf("bk_s", [128, 1], F32)
        bvrow = sbuf("bvrow", [1, 128], F32)
        bvb = sbuf("bvb", [128, 128], F32)
        bsum = sbuf("bsum_s", [128, NDC], F32)
        hmask = sbuf("hmask", [128, 2], F32)

        P.add("SP", dma(mask[:], msk_d), inc="dSP", incn=16)
        P.add("SP", dma(ident_f[:], idn_d), inc="dSP", incn=16)
        P.add("SP", dma(bq[:], bq_d), inc="dSP", incn=16)
        P.add("SP", dma(bk[:], bk_d), inc="dSP", incn=16)
        P.add("SP", dma(bsum[:], bs_d), inc="dSP", incn=16)
        P.add("SP", dma(bvrow[:], bv_d), inc="dSP", incn=16)
        P.add("SP", dma(hmask[:], hm_d), inc="dSP", incn=16)
        consts_sp = C["dSP"]
        identb_done = P.add("POOL", dma(ident_b[:], idn_d), inc="dWQ", incn=16)
        ones_done = P.add("DVE", lambda e: e.memset(onesr[:], 1.0), inc="sDVE")

        def emit_ln(xsrc, sxbuf, xdisc_ap, xnt, st, war_act, war_dve):
            P.add("SP", dma(sxbuf[:], xsrc),
                  waits=[("sACT", war_act), ("sDVE", war_dve)],
                  inc="dSP", incn=16)
            ldv = C["dSP"]
            P.add("ACT", lambda e: e.activation(
                xdisc_ap, sxbuf[:], AF.Square, accum_out=st["sq"][:]),
                waits=[("dSP", ldv)], inc="sACT")
            P.add("ACT", lambda e: e.activation(
                xdisc_ap, sxbuf[:], AF.Copy, accum_out=st["sum"][:]), inc="sACT")
            a_sum = C["sACT"]
            P.add("DVE", lambda e: e.tensor_scalar(
                st["mean"][:], st["sum"][:], 1.0 / D, None, ALU.mult),
                waits=[("sACT", a_sum)], inc="sDVE")
            d1 = C["sDVE"]
            P.add("DVE", lambda e: e.tensor_tensor(
                st["ms"][:], st["mean"][:], st["mean"][:], ALU.mult),
                waits=[("sDVE", d1)], inc="sDVE")
            d2 = C["sDVE"]
            P.add("DVE", lambda e: e.tensor_scalar(
                st["ve"][:], st["sq"][:], 1.0 / D, EPS, ALU.mult, ALU.add),
                inc="sDVE")
            d3 = C["sDVE"]
            P.add("DVE", lambda e: e.tensor_tensor(
                st["ve"][:], st["ve"][:], st["ms"][:], ALU.subtract),
                waits=[("sDVE", d3)], inc="sDVE")
            d_ve = C["sDVE"]
            P.add("ACT", lambda e: e.activation(st["lnv"][:], st["ve"][:], AF.Ln),
                  waits=[("sDVE", d_ve)], inc="sACT")
            a_ln = C["sACT"]
            P.add("ACT", lambda e: e.activation(
                st["rs"][:], st["lnv"][:], AF.Exp, scale=-0.5),
                waits=[("sACT", a_ln)], inc="sACT")
            a_rs = C["sACT"]
            P.add("DVE", lambda e: e.scalar_tensor_tensor(
                st["mrn"][:], st["mean"][:], -1.0, st["rs"][:],
                ALU.mult, ALU.mult), waits=[("sACT", a_rs)], inc="sDVE")
            d_mrn = C["sDVE"]
            P.add("DVE", lambda e: e.tensor_scalar(
                xnt[:], sxbuf[:], st["rs"][:], st["mrn"][:],
                ALU.mult, ALU.add), waits=[("sDVE", d_mrn)], inc="sDVE")
            return C["sDVE"], a_sum

        def emit_xpose(xnt, pxT, px_evac, dest, dstride, off, xnt_rdy, pe_war):
            for half in range(2):
                px = pxT[half]

                def tp(e, half=half, px=px):
                    ins = None
                    for i in range(8):
                        dc = half * 8 + i
                        ins = e.transpose(px[:, i * 128:(i + 1) * 128],
                                          xnt[:, dc * 128:(dc + 1) * 128],
                                          ident_b[:])
                    return ins
                pe_c = P.add("PE", tp,
                             waits=[("sDVE", xnt_rdy), ("dWQ", identb_done),
                                    ("sDVE", px_evac[half]), ("sPE", pe_war)],
                             inc="sPE")

                def ev(e, half=half, px=px):
                    dv = dest[:].rearrange("p (dc t) -> p dc t", t=dstride)
                    return e.tensor_copy(
                        dv[:, half * 8:(half + 1) * 8, off:off + 128],
                        px[:].rearrange("p (dc t) -> p dc t", t=128))
                P.add("DVE", ev, waits=[("sPE", pe_c)], inc="sDVE")
                px_evac[half] = C["sDVE"]

        Q_dram = nc.dram_tensor("Q_dram", [NH, 128, S], BF16, kind="Internal").ap()

        with ExitStack() as ph12:
            def sb12(name, shape, dt=F32):
                return ph12.enter_context(nc.sbuf_tensor(name, shape, dt))

            K_sb = sb12("K_sb", [128, S], BF16)
            V_sb = sb12("V_sb", [128, S], BF16)

            with ExitStack() as ph1:
                def sb1(name, shape, dt=F32):
                    return ph1.enter_context(nc.sbuf_tensor(name, shape, dt))

                sx = [sb1(f"sx{i}", [128, D]) for i in range(2)]
                xdisc = sb1("xdisc", [128, D], BF16)
                xnt = sb1("xnt", [128, D], BF16)
                xnf = [sb1(f"xnf{i}", [128, NDC * 512], BF16) for i in range(2)]
                wkv = sb1("wkv", [128, NDC * 256], BF16)
                wq8 = sb1("wq8", [128, NH * NDC * 128], BF16)
                qstg = [sb1(f"qstg{i}", [128, NH * 512], BF16) for i in range(2)]
                st1 = {n: sb1("s1_" + n, [128, 1]) for n in
                       ("sum", "sq", "mean", "ms", "ve", "rs", "mrn", "lnv")}

                pxT = [ph1.enter_context(
                    nc.psum_tensor(f"pxT{i}", [128, 1024], BF16)) for i in range(2)]
                pq = [ph1.enter_context(
                    nc.psum_tensor(f"pq{i}", [128, 512], F32)) for i in range(2)]
                pk = ph1.enter_context(nc.psum_tensor("pk", [128, 512], F32))
                pv = ph1.enter_context(nc.psum_tensor("pv", [128, 512], F32))

                kv_done = P.add("POOL", dma(wkv[:], KV1), inc="dWQ", incn=16)
                wq_done = P.add("POOL", dma(
                    wq8[:].rearrange("p (h c) -> p h c", h=NH),
                    A1.rearrange("h p c -> p h c")), inc="dWQ", incn=16)

                mm0 = P.add("PE", lambda e: e.matmul(
                    pk[:, 0:128], onesr[:], bvrow[:], start=True, stop=True),
                    waits=[("dSP", consts_sp), ("sDVE", ones_done)], inc="sPE")
                bvb_done = P.add("DVE", lambda e: e.tensor_copy(bvb[:], pk[:, 0:128]),
                                 waits=[("sPE", mm0)], inc="sDVE")

                px_evac = [0, 0]
                sx_used_a = [0, 0]
                sx_used_d = [0, 0]
                xnf_pe_used = [0, 0]
                k_evac = 0
                v_evac = 0
                q_evac = [0, 0]
                qs_dma = [0, 0]

                for sup in range(NSUP):
                    par = sup % 2
                    xs = xnf[par]
                    for tt in range(4):
                        t = sup * 4 + tt
                        rdy, asum = emit_ln(xin[t * 128:(t + 1) * 128, :],
                                            sx[t % 2], xdisc[:], xnt, st1,
                                            sx_used_a[t % 2], sx_used_d[t % 2])
                        sx_used_a[t % 2] = asum
                        sx_used_d[t % 2] = rdy
                        emit_xpose(xnt, pxT, px_evac, xs, 512, tt * 128, rdy,
                                   xnf_pe_used[par])
                    xnf_rdy = C["sDVE"]

                    def kmm(e, xs=xs):
                        ins = None
                        for dc in range(NDC):
                            ins = e.matmul(pk[:, 0:512],
                                           wkv[:, dc * 256:dc * 256 + 128],
                                           xs[:, dc * 512:(dc + 1) * 512],
                                           start=(dc == 0), stop=(dc == NDC - 1))
                        return ins
                    pe_k = P.add("PE", kmm,
                                 waits=[("sDVE", xnf_rdy), ("dWQ", kv_done),
                                        ("sDVE", k_evac), ("sDVE", bvb_done)],
                                 inc="sPE")
                    P.add("DVE", lambda e, sup=sup: e.tensor_scalar(
                        K_sb[:, sup * 512:(sup + 1) * 512], pk[:, 0:512],
                        bk[:], None, ALU.add),
                        waits=[("sPE", pe_k)], inc="sDVE")
                    k_evac = C["sDVE"]

                    def vmm(e, xs=xs):
                        ins = None
                        for tt in range(4):
                            for dc in range(NDC):
                                ins = e.matmul(
                                    pv[:, tt * 128:(tt + 1) * 128],
                                    xs[:, dc * 512 + tt * 128:dc * 512 + (tt + 1) * 128],
                                    wkv[:, dc * 256 + 128:dc * 256 + 256],
                                    start=(dc == 0), stop=(dc == NDC - 1))
                        return ins
                    pe_v = P.add("PE", vmm, waits=[("sDVE", v_evac)], inc="sPE")
                    for tt in range(4):
                        P.add("DVE", lambda e, sup=sup, tt=tt: e.tensor_tensor(
                            V_sb[:, (sup * 4 + tt) * 128:(sup * 4 + tt + 1) * 128],
                            pv[:, tt * 128:(tt + 1) * 128], bvb[:], ALU.add),
                            waits=[("sPE", pe_v)], inc="sDVE")
                    v_evac = C["sDVE"]

                    for hh in range(NH):
                        bank = pq[hh % 2]

                        def qmm(e, hh=hh, bank=bank, xs=xs):
                            ins = None
                            for dc in range(NDC):
                                ins = e.matmul(
                                    bank[:, 0:512],
                                    wq8[:, (hh * NDC + dc) * 128:(hh * NDC + dc + 1) * 128],
                                    xs[:, dc * 512:(dc + 1) * 512],
                                    start=(dc == 0), stop=(dc == NDC - 1))
                            return ins
                        pe_q = P.add("PE", qmm,
                                     waits=[("dWQ", wq_done),
                                            ("sDVE", q_evac[hh % 2])], inc="sPE")
                        P.add("DVE", lambda e, hh=hh, bank=bank, par=par: e.tensor_scalar(
                            qstg[par][:, hh * 512:(hh + 1) * 512],
                            bank[:, 0:512], bq[:, hh:hh + 1], None, ALU.add),
                            waits=[("sPE", pe_q), ("dSP", qs_dma[par])],
                            inc="sDVE")
                        q_evac[hh % 2] = C["sDVE"]
                    q_stage_rdy = C["sDVE"]
                    P.add("SP", dma(
                        Q_dram.rearrange("h p c -> p h c")[:, :, sup * 512:(sup + 1) * 512],
                        qstg[par][:].rearrange("p (h c) -> p h c", c=512)),
                        waits=[("sDVE", q_stage_rdy)], inc="dSP", incn=16)
                    qs_dma[par] = C["dSP"]
                    xnf_pe_used[par] = C["sPE"]

                kq_ready = C["sDVE"]
                q_dram_done = C["dSP"]
                if DBG:
                    P.add("SP", dma(dbg["k"], K_sb[:]), waits=[("sDVE", kq_ready)],
                          inc="dOUT", incn=16)
                    P.add("SP", dma(dbg["v"], V_sb[:]), waits=[("sDVE", kq_ready)],
                          inc="dOUT", incn=16)
                    P.add("SP", dma(dbg["q"], Q_dram[0]), waits=[("dSP", q_dram_done)],
                          inc="dOUT", incn=16)

            with ExitStack() as ph2:
                def sb2(name, shape, dt=F32):
                    return ph2.enter_context(nc.sbuf_tensor(name, shape, dt))

                Pbuf = sb2("Pbuf", [128, 2 * 32 * 512], BF16)
                Q_h = [sb2(f"Qh{i}", [128, S], BF16) for i in range(2)]
                ctx_h = [sb2(f"ctx{i}", [128, S], BF16) for i in range(2)]
                cst = [sb2(f"cst{i}", [128, S], BF16) for i in range(2)]
                ones128 = sb2("ones128", [128, 1], BF16)
                rr_sb = sb2("rr_sb", [1, 512])
                rrb = sb2("rrb", [128, 512])

                ps_s = [ph2.enter_context(
                    nc.psum_tensor(f"ps_s{i}", [128, 512], F32)) for i in range(4)]
                ps_c = [ph2.enter_context(
                    nc.psum_tensor(f"ps_c{i}", [128, 512], F32)) for i in range(2)]
                ps_r = ph2.enter_context(nc.psum_tensor("ps_r", [1, 512], F32))
                rr_ps = ph2.enter_context(nc.psum_tensor("rr_ps", [128, 512], F32))

                ones_dn = P.add("DVE", lambda e: e.memset(ones128[:], 1.0),
                                inc="sDVE")
                ctx_evac = [0, 0]
                cst_dma = [0, 0]
                p_used = [0, 0]
                q_dma = [0, 0]
                q_used = [0, 0]

                def emit_qload(h):
                    qp = h % 2
                    P.add("SP", dma(Q_h[qp][:], Q_dram[h]),
                          waits=[("sPE", q_used[qp]), ("dSP", q_dram_done)],
                          inc="dSP", incn=16)
                    q_dma[qp] = C["dSP"]

                def emit_cst(hh, ctxh_done):
                    hp = hh % 2
                    P.add("DVE", lambda e, hp=hp: e.tensor_scalar(
                        cst[hp][:], ctx_h[hp][:], hmask[:, 0:1], None, ALU.mult),
                        waits=[("sDVE", ctxh_done), ("dSP", cst_dma[hp])],
                        inc="sDVE")
                    d_c0 = C["sDVE"]
                    for d in range(2):
                        P.add("SP", dma(
                            cc_in[(d * NHEADS + hh) * 128:(d * NHEADS + hh + 1) * 128, :],
                            cst[hp][:, d * TOWN:(d + 1) * TOWN]),
                            waits=[("sDVE", d_c0)], inc="dSP", incn=16)
                    c0_dma = C["dSP"]
                    P.add("DVE", lambda e, hp=hp: e.tensor_scalar(
                        cst[hp][:], ctx_h[hp][:], hmask[:, 1:2], None, ALU.mult),
                        waits=[("dSP", c0_dma)], inc="sDVE")
                    d_c1 = C["sDVE"]
                    for d in range(2):
                        P.add("SP", dma(
                            cc_in[(d * NHEADS + NH + hh) * 128:(d * NHEADS + NH + hh + 1) * 128, :],
                            cst[hp][:, d * TOWN:(d + 1) * TOWN]),
                            waits=[("sDVE", d_c1)], inc="dSP", incn=16)
                    cst_dma[hp] = C["dSP"]

                pend = None
                mask_done = 0
                recip_hist = [0, 0]
                rrb_hist = [0, 0]

                def emit_iter(i, hh, g):
                    nonlocal pend, mask_done
                    cur = i is not None
                    if cur:
                        par = i % 2
                        qp = hh % 2
                        kcs = 4 * (g + 1)
                        pb = par * 16384
                    else:
                        kcs = 0
                    prev_exp_last = pend[4] if pend else 0

                    pairs = []
                    if pend is not None:
                        pi_, ph_, pg_, pexp, pel = pend
                        ppb = (pi_ % 2) * 16384
                        pkcs = 4 * (pg_ + 1)
                        cpar = pi_ % 2
                        for kc in range(pkcs):
                            first = (kc == 0)
                            last = (kc == pkcs - 1)

                            def rs_mm(e, kc=kc, ppb=ppb, first=first, last=last):
                                return e.matmul(
                                    ps_r[0:1, 0:512], ones128[:],
                                    Pbuf[:, ppb + kc * 512:ppb + (kc + 1) * 512],
                                    start=first, stop=last)

                            def pv_mm(e, kc=kc, ppb=ppb, cpar=cpar,
                                      first=first, last=last):
                                return e.matmul(
                                    ps_c[cpar][:, 0:512],
                                    V_sb[:, kc * 128:(kc + 1) * 128],
                                    Pbuf[:, ppb + kc * 512:ppb + (kc + 1) * 512],
                                    start=first, stop=last)
                            w1, w2 = [], []
                            if first:
                                w1 = [("sACT", pel), ("sDVE", mask_done),
                                      ("sDVE", recip_hist[1]),
                                      ("sDVE", ones_dn)]
                                w2 = [("sDVE", ctx_evac[cpar])]
                            pairs.append((rs_mm, w1, pv_mm, w2))

                    def emit_pair(pr):
                        rs_mm, w1, pv_mm, w2 = pr
                        P.add("PE", rs_mm, waits=w1, inc="sPE")
                        P.add("PE", pv_mm, waits=w2, inc="sPE")

                    exp_at = []

                    def emit_smm(kc):
                        def smm(e, kc=kc, qp=qp, g=g):
                            return e.matmul(
                                ps_s[kc % 4][:, 0:512],
                                K_sb[:, kc * 128:(kc + 1) * 128],
                                Q_h[qp][:, g * 512:(g + 1) * 512],
                                start=True, stop=True)
                        w = []
                        if kc == 0:
                            w = [("sACT", prev_exp_last), ("sDVE", kq_ready),
                                 ("dSP", q_dma[qp])]
                        elif kc >= 4:
                            w = [("sACT", exp_at[kc - 4])]
                        pe_s = P.add("PE", smm, waits=w, inc="sPE")
                        ew = [("sPE", pe_s)]
                        if kc == 0:
                            ew.append(("sPE", p_used[par]))
                        P.add("ACT", lambda e, kc=kc, pb=pb: e.activation(
                            Pbuf[:, pb + kc * 512:pb + (kc + 1) * 512],
                            ps_s[kc % 4][:, 0:512], AF.Exp, scale=ALPHA),
                            waits=ew, inc="sACT")
                        exp_at.append(C["sACT"])

                    pidx = 0
                    for kc in range(min(4, kcs)):
                        emit_smm(kc)
                    for kc in range(4, kcs):
                        if pidx < len(pairs):
                            emit_pair(pairs[pidx])
                            pidx += 1
                        emit_smm(kc)
                    while pidx < len(pairs):
                        emit_pair(pairs[pidx])
                        pidx += 1

                    exp_last = C["sACT"] if cur else prev_exp_last
                    if cur and g == NSUP - 1:
                        q_used[qp] = C["sPE"]

                    if pend is not None:
                        pi_, ph_, pg_, pexp, pel = pend
                        cpar = pi_ % 2
                        hp = ph_ % 2
                        pv_last = C["sPE"]
                        p_used[cpar] = pv_last
                        P.add("DVE", lambda e: e.reciprocal(
                            rr_sb[:], ps_r[0:1, 0:512]),
                            waits=[("sPE", pv_last)], inc="sDVE")
                        rc = C["sDVE"]
                        recip_hist[1] = recip_hist[0]
                        recip_hist[0] = rc
                        P.add("PE", lambda e: e.matmul(
                            rr_ps[:, 0:512], onesr[:], rr_sb[:],
                            start=True, stop=True),
                            waits=[("sDVE", rc), ("sDVE", rrb_hist[1])],
                            inc="sPE")
                        bc = C["sPE"]
                        P.add("DVE", lambda e: e.tensor_copy(
                            rrb[:], rr_ps[:, 0:512]),
                            waits=[("sPE", bc)], inc="sDVE")
                        rrb_hist[1] = rrb_hist[0]
                        rrb_hist[0] = C["sDVE"]
                        P.add("DVE", lambda e, pg_=pg_, cpar=cpar, hp=hp: e.tensor_tensor(
                            ctx_h[hp][:, pg_ * 512:(pg_ + 1) * 512],
                            ps_c[cpar][:, 0:512], rrb[:], ALU.mult),
                            inc="sDVE")
                        ctx_evac[cpar] = C["sDVE"]
                        if pg_ == NSUP - 1:
                            emit_cst(ph_, ctx_evac[cpar])

                    if cur:
                        P.add("DVE", lambda e, pb=pb, g=g: e.tensor_tensor(
                            Pbuf[:, pb + 4 * g * 512:pb + 4 * g * 512 + 2048],
                            Pbuf[:, pb + 4 * g * 512:pb + 4 * g * 512 + 2048],
                            mask[:], ALU.mult),
                            waits=[("sACT", exp_last), ("dSP", consts_sp)],
                            inc="sDVE")
                        mask_done = C["sDVE"]
                        pend = (i, hh, g, exp_at, exp_last)
                    else:
                        pend = None

                emit_qload(0)
                if NH > 1:
                    emit_qload(1)
                flat = [(h, g) for h in range(NH) for g in range(NSUP)]
                for i, (hh, g) in enumerate(flat):
                    if g == 2 and 2 <= hh + 1 < NH:
                        emit_qload(hh + 1)
                    emit_iter(i, hh, g)
                emit_iter(None, None, None)
                ctx_writes = C["dSP"]

        def coll(e):
            return e.collective_compute(
                "ReduceScatter", ALU.add,
                replica_groups=[[0, 1], [2, 3], [4, 5], [6, 7]],
                ins=[cc_in.opt()], outs=[cc_out.opt()])
        P.add("POOL", coll, waits=[("dSP", ctx_writes)], inc="sPOOL")
        cc_done = C["sPOOL"]
        if DBG:
            P.add("POOL", dma(dbg["cco"], cc_out), waits=[("sPOOL", cc_done)],
                  inc="dOUT", incn=16)

        P.barrier()

        CAP = 768
        NST = CAP // 128
        with ExitStack() as ph3:
            def sb3(name, shape, dt=F32):
                return ph3.enter_context(nc.sbuf_tensor(name, shape, dt))

            sxo = [sb3(f"sxo{i}", [128, D]) for i in range(2)]
            xnf_t = sb3("xnf_t", [128, D], BF16)
            gates_p = sb3("gates_p", [128, 16 * NE], F32)
            lg = sb3("lg", [128, 16 * NE], F32)
            m8 = sb3("m8", [128, 16 * NE], F32)
            e1m = sb3("e1m", [128, NE], F32)
            e2m = sb3("e2m", [128, NE], F32)
            gd = sb3("gd", [128, 16], F32)
            g1 = sb3("g1", [128, 16], F32)
            g2 = sb3("g2", [128, 16], F32)
            gsig = sb3("gsig", [128, 16], F32)
            r1s3 = sb3("r1s3", [128, NDC * 8], BF16)
            st3 = {n: sb3("s3_" + n, [128, 1]) for n in
                   ("sum", "sq", "mean", "ms", "ve", "rs", "mrn", "lnv")}
            mskb = sb3("mskb", [128, 16], F32)
            zz16 = sb3("zz16", [128, 16], F32)
            csum = sb3("csum", [128, 16], F32)
            rcnt = sb3("rcnt", [128, 1], F32)
            baseb = sb3("baseb", [128, 1], F32)
            slotb = sb3("slotb", [128, 16], F32)
            posf = sb3("posf", [128, 16], F32)
            d1f = sb3("d1f", [128, 16], F32)
            d2f = sb3("d2f", [128, 16], F32)
            destt = sb3("destt", [128, 16], F32)
            dtmp = sb3("dtmp", [128, 16], F32)
            over = sb3("over", [128, 16], F32)
            d1i = sb3("d1i", [128, 16], I32)
            scn8 = sb3("scn8", [128, 8], F32)
            u8 = sb3("u8", [128, 8], F32)
            kcol = sb3("kcol", [128, 1], F32)
            d2i = sb3("d2i", [128, 16], I32)
            e1all = sb3("e1all", [128, 16 * NE], F32)
            e2all = sb3("e2all", [128, 16 * NE], F32)
            zbf = sb3("zbf", [128, D], BF16)
            brow = sb3("brow", [1, D], F32)
            zf32 = sb3("zf32", [1, D], F32)
            pstg = [sb3(f"pstg{i}", [128, 512], F32) for i in range(2)]

            r1_done = P.add("POOL", dma(r1s3[:], R1), inc="dWQ", incn=16)
            idx_consts = P.add("SP", dma(brow[:], bsr_d), inc="dSP", incn=16)

            with ExitStack() as pro:
                xnTM = pro.enter_context(
                    nc.sbuf_tensor("xnTM", [128, 16 * D], BF16))
                tri_sb = pro.enter_context(nc.sbuf_tensor("tri_sb", [128, 128], F32))
                P.add("SP", dma(tri_sb[:], tri_d), inc="dSP", incn=16)
                idx_consts2 = C["dSP"]
                pxT3 = [pro.enter_context(nc.psum_tensor(
                    f"px3_{i}", [128, 1024], BF16)) for i in range(2)]
                plg = pro.enter_context(nc.psum_tensor("plg", [128, 512], F32))

                px_evac3 = [0, 0]
                sxo_a = [0, 0]
                sxo_d = [0, 0]
                lg_evac = 0
                rt_pe = 0
                for tt in range(16):
                    spar = tt % 2
                    xslice = xnTM[:, tt * D:(tt + 1) * D]
                    rdy, asum = emit_ln(xown[tt * 128:(tt + 1) * 128, :],
                                        sxo[spar], zbf[:], xslice, st3,
                                        sxo_a[spar], sxo_d[spar])
                    sxo_a[spar] = asum
                    sxo_d[spar] = rdy
                    for half in range(2):
                        px = pxT3[half]

                        def tp3(e, half=half, px=px, xslice=xslice):
                            ins = None
                            for i in range(8):
                                dc = half * 8 + i
                                ins = e.transpose(
                                    px[:, i * 128:(i + 1) * 128],
                                    xslice[:, dc * 128:(dc + 1) * 128],
                                    ident_b[:])
                            return ins
                        pe_c = P.add("PE", tp3,
                                     waits=[("sDVE", rdy), ("sDVE", px_evac3[half]),
                                            ("dWQ", identb_done)], inc="sPE")

                        def ev3(e, half=half, px=px):
                            dv = xnf_t[:].rearrange("p (dc t) -> p dc t", t=128)
                            return e.tensor_copy(
                                dv[:, half * 8:(half + 1) * 8, :],
                                px[:].rearrange("p (dc t) -> p dc t", t=128))
                        P.add("DVE", ev3, waits=[("sPE", pe_c), ("sPE", rt_pe)],
                              inc="sDVE")
                        px_evac3[half] = C["sDVE"]
                    xt_rdy = C["sDVE"]

                    def rmm(e):
                        ins = None
                        for dc in range(NDC):
                            ins = e.matmul(
                                plg[:, 0:8],
                                xnf_t[:, dc * 128:(dc + 1) * 128],
                                r1s3[:, dc * 8:(dc + 1) * 8],
                                start=(dc == 0), stop=(dc == NDC - 1))
                        return ins
                    pe_r = P.add("PE", rmm,
                                 waits=[("sDVE", xt_rdy), ("dWQ", r1_done),
                                        ("sDVE", lg_evac)], inc="sPE")
                    rt_pe = pe_r
                    P.add("DVE", lambda e, tt=tt: e.tensor_copy(
                        lg[:, tt * 8:(tt + 1) * 8], plg[:, 0:8]),
                        waits=[("sPE", pe_r)], inc="sDVE")
                    lg_evac = C["sDVE"]
                xnTM_rdy = C["sDVE"]
                P.add("DVE", lambda e: e.memset(zz16[:], 0.0), inc="sDVE")

                for tt in range(16):
                    P.add("DVE", lambda e, tt=tt: e.max(
                        m8[:, tt * 8:(tt + 1) * 8], lg[:, tt * 8:(tt + 1) * 8]),
                        waits=[("sDVE", xnTM_rdy)], inc="sDVE")
                d_m8 = C["sDVE"]
                m8v = m8[:].rearrange("p (t e) -> p t e", e=8)
                P.add("DVE", lambda e: e.tensor_tensor(
                    gd[:], m8v[:, :, 0], m8v[:, :, 1], ALU.subtract),
                    waits=[("sDVE", d_m8)], inc="sDVE")
                d_gd = C["sDVE"]
                P.add("ACT", lambda e: e.activation(
                    gsig[:], gd[:], AF.Exp, scale=-1.0),
                    waits=[("sDVE", d_gd)], inc="sACT")
                a_gs = C["sACT"]
                P.add("DVE", lambda e: e.tensor_scalar(
                    gsig[:], gsig[:], 1.0, None, ALU.add),
                    waits=[("sACT", a_gs)], inc="sDVE")
                d_g1p = C["sDVE"]
                P.add("DVE", lambda e: e.reciprocal(g1[:], gsig[:]),
                      waits=[("sDVE", d_g1p)], inc="sDVE")
                d_g1 = C["sDVE"]
                P.add("DVE", lambda e: e.tensor_scalar(
                    g2[:], g1[:], -1.0, 1.0, ALU.mult, ALU.add),
                    waits=[("sDVE", d_g1)], inc="sDVE")
                d_g2 = C["sDVE"]
                P.add("DVE", lambda e: e.nop(), waits=[("sDVE", d_g2)])
                for tt in range(16):
                    sl = (tt * 8, (tt + 1) * 8)
                    P.add("DVE", lambda e, sl=sl, tt=tt: e.tensor_scalar(
                        e1m[:], lg[:, sl[0]:sl[1]],
                        m8[:, tt * 8:tt * 8 + 1], None, ALU.is_equal),
                        inc="sDVE")
                    d0 = C["sDVE"]
                    P.add("DVE", lambda e: e.tensor_tensor_scan(
                        scn8[:], e1m[:], zz16[:, 0:8], 0.0, ALU.add, ALU.add),
                        waits=[("sDVE", d0)], inc="sDVE")
                    d1_ = C["sDVE"]
                    P.add("DVE", lambda e: e.tensor_scalar(
                        u8[:], scn8[:], 1.0, None, ALU.is_equal),
                        waits=[("sDVE", d1_)], inc="sDVE")
                    d2_ = C["sDVE"]
                    P.add("DVE", lambda e, sl=sl: e.tensor_tensor(
                        e1all[:, sl[0]:sl[1]], e1m[:], u8[:], ALU.mult),
                        waits=[("sDVE", d2_)], inc="sDVE")
                    P.add("DVE", lambda e, sl=sl, tt=tt: e.tensor_scalar(
                        e2m[:], lg[:, sl[0]:sl[1]],
                        m8[:, tt * 8 + 1:tt * 8 + 2], None, ALU.is_equal),
                        inc="sDVE")
                    d3_ = C["sDVE"]
                    P.add("DVE", lambda e: e.tensor_tensor_scan(
                        scn8[:], e2m[:], zz16[:, 0:8], 0.0, ALU.add, ALU.add),
                        waits=[("sDVE", d3_)], inc="sDVE")
                    P.add("DVE", lambda e, tt=tt: e.tensor_tensor(
                        kcol[:], m8[:, tt * 8:tt * 8 + 1],
                        m8[:, tt * 8 + 1:tt * 8 + 2], ALU.is_equal),
                        inc="sDVE")
                    d4_ = C["sDVE"]
                    P.add("DVE", lambda e: e.tensor_scalar(
                        kcol[:], kcol[:], 1.0, None, ALU.add),
                        waits=[("sDVE", d4_)], inc="sDVE")
                    d5_ = C["sDVE"]
                    P.add("DVE", lambda e: e.tensor_scalar(
                        u8[:], scn8[:], kcol[:], None, ALU.is_equal),
                        waits=[("sDVE", d5_)], inc="sDVE")
                    d6_ = C["sDVE"]
                    P.add("DVE", lambda e, sl=sl: e.tensor_tensor(
                        e2all[:, sl[0]:sl[1]], e2m[:], u8[:], ALU.mult),
                        waits=[("sDVE", d6_)], inc="sDVE")
                    d_e = C["sDVE"]
                    P.add("DVE", lambda e, sl=sl, tt=tt: e.scalar_tensor_tensor(
                        e1m[:], e1all[:, sl[0]:sl[1]], g1[:, tt:tt + 1],
                        zz16[:, 0:8], ALU.mult, ALU.add),
                        waits=[("sDVE", d_e)], inc="sDVE")
                    d_e1 = C["sDVE"]
                    P.add("DVE", lambda e, sl=sl, tt=tt: e.scalar_tensor_tensor(
                        gates_p[:, sl[0]:sl[1]],
                        e2all[:, sl[0]:sl[1]], g2[:, tt:tt + 1],
                        e1m[:], ALU.mult, ALU.add),
                        waits=[("sDVE", d_e1)], inc="sDVE")
                gates_rdy = C["sDVE"]

                P.add("DVE", lambda e: e.memset(zbf[:], 0.0),
                      waits=[("sACT", C["sACT"])], inc="sDVE")
                P.add("DVE", lambda e: e.memset(zf32[:], 0.0), inc="sDVE")
                z_rdy = C["sDVE"]
                for e_i in range(NE):
                    for st_ in range(NST):
                        P.add("POOL", dma(
                            xe_d[e_i * CAP + st_ * 128:e_i * CAP + (st_ + 1) * 128, :],
                            zbf[:]), waits=[("sDVE", z_rdy)], inc="dWQ", incn=16)
                P.add("POOL", dma(o_d[NE * CAP:NE * CAP + 1, :], zf32[:]),
                      waits=[("sDVE", z_rdy)], inc="dWQ", incn=16)
                zf_done = C["dWQ"]

                TRASH = float(NE * CAP)
                P.add("DVE", lambda e: e.memset(d1f[:], 0.0),
                      waits=[("sDVE", gates_rdy)], inc="sDVE")
                P.add("DVE", lambda e: e.memset(d2f[:], 0.0), inc="sDVE")
                e1v = e1all[:].rearrange("p (t e) -> p t e", e=8)
                e2v = e2all[:].rearrange("p (t e) -> p t e", e=8)
                gpv = gates_p[:].rearrange("p (t e) -> p t e", e=8)
                for e_i in range(NE):
                    P.add("DVE", lambda e, e_i=e_i: e.tensor_scalar(
                        mskb[:], gpv[:, :, e_i], 0.0, None, ALU.is_gt),
                        waits=[("sDVE", gates_rdy)], inc="sDVE")
                    d_mk = C["sDVE"]
                    P.add("DVE", lambda e: e.tensor_tensor_scan(
                        csum[:], mskb[:], zz16[:], 0.0, ALU.add, ALU.add),
                        waits=[("sDVE", d_mk)], inc="sDVE")
                    d_cs = C["sDVE"]
                    P.add("DVE", lambda e: e.tensor_copy(
                        rcnt[:], csum[:, 15:16]),
                        waits=[("sDVE", d_cs)], inc="sDVE")
                    d_rc = C["sDVE"]
                    pe_tr = P.add("PE", lambda e: e.matmul(
                        plg[:, 0:1], tri_sb[:], rcnt[:], start=True, stop=True),
                        waits=[("sDVE", d_rc), ("dSP", idx_consts2)], inc="sPE")
                    P.add("ACT", lambda e: e.copy(baseb[:], plg[:, 0:1]),
                          waits=[("sPE", pe_tr)], inc="sACT")
                    a_bs = C["sACT"]
                    P.add("DVE", lambda e: e.tensor_tensor(
                        slotb[:], csum[:], mskb[:], ALU.subtract),
                        waits=[("sACT", a_bs)], inc="sDVE")
                    d_sl = C["sDVE"]
                    P.add("DVE", lambda e, e_i=e_i: e.tensor_scalar(
                        slotb[:], slotb[:], baseb[:], float(e_i * CAP),
                        ALU.add, ALU.add),
                        waits=[("sDVE", d_sl)], inc="sDVE")
                    d_s2 = C["sDVE"]
                    P.add("DVE", lambda e, e_i=e_i: e.tensor_scalar(
                        over[:], slotb[:], float((e_i + 1) * CAP), None,
                        ALU.is_ge), waits=[("sDVE", d_s2)], inc="sDVE")
                    P.add("DVE", lambda e: e.tensor_scalar(
                        dtmp[:], slotb[:], -1.0, TRASH, ALU.mult, ALU.add),
                        inc="sDVE")
                    d_s3 = C["sDVE"]
                    P.add("DVE", lambda e: e.tensor_tensor(
                        dtmp[:], dtmp[:], over[:], ALU.mult),
                        waits=[("sDVE", d_s3)], inc="sDVE")
                    d_s4 = C["sDVE"]
                    P.add("DVE", lambda e: e.tensor_tensor(
                        destt[:], slotb[:], dtmp[:], ALU.add),
                        waits=[("sDVE", d_s4)], inc="sDVE")
                    d_de0 = C["sDVE"]
                    P.add("DVE", lambda e: e.tensor_scalar(
                        dtmp[:], destt[:], -1.0, TRASH, ALU.mult, ALU.add),
                        waits=[("sDVE", d_de0)], inc="sDVE")
                    P.add("DVE", lambda e: e.tensor_scalar(
                        over[:], mskb[:], -1.0, 1.0, ALU.mult, ALU.add),
                        inc="sDVE")
                    d_de1 = C["sDVE"]
                    P.add("DVE", lambda e: e.tensor_tensor(
                        dtmp[:], dtmp[:], over[:], ALU.mult),
                        waits=[("sDVE", d_de1)], inc="sDVE")
                    d_de2 = C["sDVE"]
                    P.add("DVE", lambda e: e.tensor_tensor(
                        destt[:], destt[:], dtmp[:], ALU.add),
                        waits=[("sDVE", d_de2)], inc="sDVE")
                    d_de = C["sDVE"]
                    P.add("DVE", lambda e, e_i=e_i: e.tensor_tensor(
                        dtmp[:], destt[:], e1v[:, :, e_i], ALU.mult),
                        waits=[("sDVE", d_de)], inc="sDVE")
                    d_p1 = C["sDVE"]
                    P.add("DVE", lambda e: e.tensor_tensor(
                        d1f[:], d1f[:], dtmp[:], ALU.add),
                        waits=[("sDVE", d_p1)], inc="sDVE")
                    P.add("DVE", lambda e, e_i=e_i: e.tensor_tensor(
                        dtmp[:], destt[:], e2v[:, :, e_i], ALU.mult),
                        inc="sDVE")
                    d_p2 = C["sDVE"]
                    P.add("DVE", lambda e: e.tensor_tensor(
                        d2f[:], d2f[:], dtmp[:], ALU.add),
                        waits=[("sDVE", d_p2)], inc="sDVE")
                d_dd = C["sDVE"]
                P.add("DVE", lambda e: e.tensor_copy(d1i[:], d1f[:]),
                      waits=[("sDVE", d_dd)], inc="sDVE")
                P.add("DVE", lambda e: e.tensor_copy(d2i[:], d2f[:]),
                      inc="sDVE")
                posi_rdy = C["sDVE"]
                P.add("DVE", lambda e: e.nop(), waits=[("sDVE", posi_rdy)],
                      inc="sDVE")
                posi_rdy = C["sDVE"]

                for tt in range(16):
                    for dcol in (d1i, d2i):
                        P.add("POOL", lambda e, tt=tt, dcol=dcol: e.indirect_dma_start(
                            out=xe_d,
                            out_offset=IndirectOffsetOnAxis(dcol[:, tt:tt + 1], 0),
                            in_=xnTM[:, tt * D:(tt + 1) * D], in_offset=None,
                            bounds_check=None, oob_is_err=False),
                            waits=[("sDVE", posi_rdy), ("dWQ", zf_done)],
                            inc="dWQ", incn=16)
                scat_done = C["dWQ"]

            act_pro = C["sACT"]
            dve_pro = C["sDVE"]
            with ExitStack() as prj:
                ctxh = prj.enter_context(
                    nc.sbuf_tensor("ctxh", [128, NDC * 1024], BF16))
                p5b = [prj.enter_context(nc.psum_tensor(
                    f"p5p{i}", [128, 512], F32)) for i in range(0)]
                p5s = [prj.enter_context(nc.sbuf_tensor(
                    f"p5s{i}", [128, NDC * 512], BF16)) for i in range(2)]
                po_p = [prj.enter_context(nc.psum_tensor(
                    f"po_p{i}", [128, 512], F32)) for i in range(2)]

                ctx_ld = 0
                p5_used = [0, 0]
                po_free = [0, 0]
                pstg_used = [0, 0]
                acc_w = 0
                for th in range(2):
                    for dc in range(NDC):
                        P.add("SP", dma(
                            ctxh[:, dc * 1024:(dc + 1) * 1024],
                            cc_out[dc * 128:(dc + 1) * 128,
                                   th * 1024:(th + 1) * 1024]),
                            waits=[("sPOOL", cc_done), ("dWQ", scat_done),
                                   ("sPE", C["sPE"] if th == 0 else 0)],
                            inc="dSP", incn=16)
                    ctx_ld = C["dSP"]
                    for dg in range(4):
                        wpar = dg % 2
                        P.add("POOL", dma(
                            p5s[wpar][:].rearrange("p (dc c) -> p dc c", c=512),
                            P5[:, :, dg * 512:(dg + 1) * 512].rearrange(
                                "dc p c -> p dc c")),
                            waits=[("sPE", p5_used[wpar]),
                                   ("dWQ", scat_done)], inc="dWQ", incn=16)
                        wv = C["dWQ"]
                        for t8 in range(8):
                            opar = t8 % 2

                            def pmm(e, th=th, dg=dg, t8=t8, wpar=wpar, opar=opar):
                                ins = None
                                for dc in range(NDC):
                                    ins = e.matmul(
                                        po_p[opar][:, 0:512],
                                        ctxh[:, dc * 1024 + t8 * 128:dc * 1024 + (t8 + 1) * 128],
                                        p5s[wpar][:, dc * 512:(dc + 1) * 512],
                                        start=(dc == 0), stop=(dc == NDC - 1))
                                return ins
                            pe_p = P.add("PE", pmm,
                                         waits=[("dWQ", wv), ("dSP", ctx_ld),
                                                ("sACT", act_pro),
                                                ("sDVE", dve_pro),
                                                ("sDVE", po_free[opar])],
                                         inc="sPE")
                            P.add("DVE", lambda e, opar=opar: e.tensor_copy(
                                pstg[opar][:], po_p[opar][:, 0:512]),
                                waits=[("sPE", pe_p), ("dSP", pstg_used[opar])],
                                inc="sDVE")
                            po_free[opar] = C["sDVE"]
                            d_ev = C["sDVE"]
                            P.add("SP", lambda e, th=th, dg=dg, t8=t8, opar=opar: e.dma_start(
                                out=acc_d[th * 1024 + t8 * 128:th * 1024 + (t8 + 1) * 128,
                                          dg * 512:(dg + 1) * 512],
                                in_=pstg[opar][:]),
                                waits=[("sDVE", d_ev)], inc="dSP", incn=16)
                            pstg_used[opar] = C["dSP"]
                        p5_used[wpar] = C["sPE"]
                proj_acc_done = C["dSP"]
            pe_prj = C["sPE"]
    
            act_prj = C["sACT"]
            dve_prj = C["sDVE"]
            dsp_prj = C["dSP"]

            with ExitStack() as mainsc:
                def sbm(name, shape, dt=F32):
                    return mainsc.enter_context(nc.sbuf_tensor(name, shape, dt))
                xet = [sbm(f"xet{i}", [128, D], BF16) for i in range(2)]
                xef = sbm("xef", [128, NDC * CAP], BF16)
                w_in = [sbm(f"w_in{i}", [128, NDC * 256], BF16) for i in range(2)]
                w4 = [sbm(f"w4_{i}", [128, 32 * 256], BF16) for i in range(2)]
                g_sb = sbm("g_sb", [128, 32 * CAP], BF16)
                osb = [sbm(f"osb{i}", [128, NST * 256], F32) for i in range(2)]
                sA = [sbm(f"sA{i}", [128, 512], F32) for i in range(2)]
                pab = [mainsc.enter_context(nc.psum_tensor(
                    f"pab{i}", [128, 512], F32)) for i in range(4)]
                po = [mainsc.enter_context(nc.psum_tensor(
                    f"po{i}", [128, 512], F32)) for i in range(2)]
                pxe = [mainsc.enter_context(nc.psum_tensor(
                    f"pxe{i}", [128, 1024], BF16)) for i in range(2)]

                ab_free_a = [act_prj, act_prj]
                ab_free_d = [dve_prj, dve_prj]
                po_free = [dve_prj, dve_prj]
                w_in_used = [pe_prj, pe_prj]
                w4_used = [pe_prj, pe_prj]
                sA_used = [dve_prj, dve_prj]
                xet_used = [pe_prj, pe_prj]
                pxe_evac = [dve_prj, dve_prj]
                xef_rdy = dve_prj
                osb_used = [dsp_prj, dsp_prj]
                arena_pe = pe_prj

                def emit_xef(e_i):
                    nonlocal xef_rdy
                    last = 0
                    for st_ in range(NST):
                        tp_ = st_ % 2
                        P.add("SP", dma(
                            xet[tp_][:],
                            xe_d[e_i * CAP + st_ * 128:e_i * CAP + (st_ + 1) * 128, :]),
                            waits=[("sPE", xet_used[tp_]), ("dWQ", scat_done)],
                            inc="dSP", incn=16)
                        xv = C["dSP"]
                        for half in range(2):
                            px = pxe[half]

                            def tpx(e, half=half, px=px, tp_=tp_):
                                ins = None
                                for i in range(8):
                                    dc = half * 8 + i
                                    ins = e.transpose(
                                        px[:, i * 128:(i + 1) * 128],
                                        xet[tp_][:, dc * 128:(dc + 1) * 128],
                                        ident_b[:])
                                return ins
                            pe_c = P.add("PE", tpx,
                                         waits=[("dSP", xv), ("sDVE", pxe_evac[half])],
                                         inc="sPE")
                            xet_used[tp_] = C["sPE"]

                            def evx(e, half=half, px=px, st_=st_):
                                dv = xef[:].rearrange("p (dc t) -> p dc t", t=CAP)
                                return e.tensor_copy(
                                    dv[:, half * 8:(half + 1) * 8,
                                       st_ * 128:(st_ + 1) * 128],
                                    px[:].rearrange("p (dc t) -> p dc t", t=128))
                            P.add("DVE", evx, waits=[("sPE", pe_c)], inc="sDVE")
                            pxe_evac[half] = C["sDVE"]
                        last = C["sDVE"]
                    xef_rdy = last

                for e_i in range(NE):
                    emit_xef(e_i)
                    xef_ok = xef_rdy
                    for j in range(32):
                        wpar = j % 2
                        P.add("POOL", dma(w_in[wpar][:], W1[e_i * 32 + j]),
                              waits=[("sPE", w_in_used[wpar])], inc="dWQ", incn=16)
                        wv = C["dWQ"]
                        for tg, (o0, n) in enumerate(((0, 512), (512, 256))):
                            ab_a = pab[2 * tg]
                            ab_b = pab[2 * tg + 1]

                            def amm(e, wpar=wpar, o0=o0, n=n, ab_a=ab_a):
                                ins = None
                                for dc in range(NDC):
                                    ins = e.matmul(
                                        ab_a[:, 0:n],
                                        w_in[wpar][:, dc * 256:dc * 256 + 128],
                                        xef[:, dc * CAP + o0:dc * CAP + o0 + n],
                                        start=(dc == 0), stop=(dc == NDC - 1))
                                return ins
                            pe_a = P.add("PE", amm,
                                         waits=[("dWQ", wv), ("sDVE", xef_ok),
                                                ("sACT", ab_free_a[tg])],
                                         inc="sPE")

                            def bmm(e, wpar=wpar, o0=o0, n=n, ab_b=ab_b):
                                ins = None
                                for dc in range(NDC):
                                    ins = e.matmul(
                                        ab_b[:, 0:n],
                                        w_in[wpar][:, dc * 256 + 128:dc * 256 + 256],
                                        xef[:, dc * CAP + o0:dc * CAP + o0 + n],
                                        start=(dc == 0), stop=(dc == NDC - 1))
                                return ins
                            pe_b2 = P.add("PE", bmm,
                                          waits=[("sDVE", ab_free_d[tg])],
                                          inc="sPE")
                            P.add("ACT", lambda e, ab_a=ab_a, tg=tg, n=n: e.activation(
                                sA[tg][:, 0:n], ab_a[:, 0:n], AF.Silu),
                                waits=[("sPE", pe_a), ("sDVE", sA_used[tg])],
                                inc="sACT")
                            ab_free_a[tg] = C["sACT"]
                            a_sil = C["sACT"]
                            P.add("DVE", lambda e, ab_b=ab_b, tg=tg, j=j, o0=o0, n=n: e.tensor_tensor(
                                g_sb[:, j * CAP + o0:j * CAP + o0 + n],
                                sA[tg][:, 0:n], ab_b[:, 0:n], ALU.mult),
                                waits=[("sPE", pe_b2), ("sACT", a_sil),
                                       ("sPE", arena_pe)],
                                inc="sDVE")
                            ab_free_d[tg] = C["sDVE"]
                            sA_used[tg] = C["sDVE"]
                        w_in_used[wpar] = C["sPE"]
                    g_done = C["sDVE"]

                    for dq in range(8):
                        wpar = dq % 2
                        P.add("POOL", dma(
                            w4[wpar][:].rearrange("p (j c) -> p j c", c=256),
                            M4[e_i * 32:(e_i + 1) * 32, :,
                               dq * 256:(dq + 1) * 256].rearrange("j p c -> p j c")),
                            waits=[("sPE", w4_used[wpar])], inc="dWQ", incn=16)
                        wv = C["dWQ"]
                        obpar = dq % 2
                        for st_ in range(NST):
                            po_par = st_ % 2

                            def omm(e, wpar=wpar, st_=st_, po_par=po_par):
                                ins = None
                                for j in range(32):
                                    ins = e.matmul(
                                        po[po_par][:, 0:256],
                                        g_sb[:, j * CAP + st_ * 128:j * CAP + (st_ + 1) * 128],
                                        w4[wpar][:, j * 256:(j + 1) * 256],
                                        start=(j == 0), stop=(j == 31))
                                return ins
                            pe_o = P.add("PE", omm,
                                         waits=[("dWQ", wv), ("sDVE", g_done),
                                                ("sDVE", po_free[po_par])],
                                         inc="sPE")
                            P.add("DVE", lambda e, st_=st_, obpar=obpar, po_par=po_par: e.tensor_copy(
                                osb[obpar][:, st_ * 256:(st_ + 1) * 256],
                                po[po_par][:, 0:256]),
                                waits=[("sPE", pe_o), ("dSP", osb_used[obpar])],
                                inc="sDVE")
                            po_free[po_par] = C["sDVE"]
                        o_ev = C["sDVE"]
                        for st_ in range(NST):
                            P.add("SP", lambda e, e_i=e_i, dq=dq, st_=st_, obpar=obpar: e.dma_start(
                                out=o_d[e_i * CAP + st_ * 128:e_i * CAP + (st_ + 1) * 128,
                                        dq * 256:(dq + 1) * 256],
                                in_=osb[obpar][:, st_ * 256:(st_ + 1) * 256]),
                                waits=[("sDVE", o_ev)], inc="dSP", incn=16)
                        osb_used[obpar] = C["dSP"]
                        w4_used[wpar] = C["sPE"]
                    arena_pe = C["sPE"]
                o_writes = C["dSP"]
                adds_done = o_writes

            with ExitStack() as epi:
                ep = [epi.enter_context(nc.sbuf_tensor(
                    f"ep{i}", [128, D], F32)) for i in range(2)]
                xo2 = [epi.enter_context(nc.sbuf_tensor(
                    f"xo2_{i}", [128, D], F32)) for i in range(2)]
                stg = [epi.enter_context(nc.sbuf_tensor(
                    f"stg{i}", [128, D], F32)) for i in range(2)]
                bias_bc = epi.enter_context(nc.sbuf_tensor(
                    "bias_bc", [128, D], F32))
                pbe = epi.enter_context(nc.psum_tensor("pbe", [128, 512], F32))
                for half in range(4):
                    mmb = P.add("PE", lambda e, half=half: e.matmul(
                        pbe[:, 0:512], onesr[:],
                        brow[0:1, half * 512:(half + 1) * 512],
                        start=True, stop=True),
                        waits=[("dSP", idx_consts), ("sDVE", ones_done),
                               ("sDVE", C["sDVE"]), ("sPE", arena_pe)], inc="sPE")
                    P.add("DVE", lambda e, half=half: e.tensor_copy(
                        bias_bc[:, half * 512:(half + 1) * 512], pbe[:, 0:512]),
                        waits=[("sPE", mmb)], inc="sDVE")
                bias_bc_done = C["sDVE"]
                ep_used = [0, 0]
                xo_used = [0, 0]
                stg_used = [0, 0]
                ost_dma = [0, 0]
                for tt in range(16):
                    par = tt % 2
                    P.add("SP", dma(ep[par][:],
                                    acc_d[tt * 128:(tt + 1) * 128, :]),
                          waits=[("dSP", adds_done), ("sDVE", ep_used[par]),
                                 ("dSP", proj_acc_done)],
                          inc="dSP", incn=16)
                    av = C["dSP"]
                    P.add("SP", dma(xo2[par][:],
                                    xown[tt * 128:(tt + 1) * 128, :]),
                          waits=[("sDVE", xo_used[par])], inc="dSP", incn=16)
                    xv = C["dSP"]
                    P.add("POOL", lambda e, tt=tt, par=par: e.indirect_dma_start(
                        out=stg[par][:], out_offset=None,
                        in_=o_d,
                        in_offset=IndirectOffsetOnAxis(d1i[:, tt:tt + 1], 0),
                        bounds_check=None, oob_is_err=False),
                        waits=[("dSP", o_writes), ("sDVE", stg_used[par])],
                        inc="dWQ", incn=16)
                    g1v = C["dWQ"]
                    P.add("DVE", lambda e, tt=tt, par=par: e.scalar_tensor_tensor(
                        ep[par][:], stg[par][:], g1[:, tt:tt + 1], ep[par][:],
                        ALU.mult, ALU.add),
                        waits=[("dWQ", g1v), ("dSP", av), ("dOUT", ost_dma[par])],
                        inc="sDVE")
                    d_a1 = C["sDVE"]
                    P.add("POOL", lambda e, tt=tt, par=par: e.indirect_dma_start(
                        out=stg[par][:], out_offset=None,
                        in_=o_d,
                        in_offset=IndirectOffsetOnAxis(d2i[:, tt:tt + 1], 0),
                        bounds_check=None, oob_is_err=False),
                        waits=[("sDVE", d_a1)], inc="dWQ", incn=16)
                    g2v = C["dWQ"]
                    P.add("DVE", lambda e, tt=tt, par=par: e.scalar_tensor_tensor(
                        ep[par][:], stg[par][:], g2[:, tt:tt + 1], ep[par][:],
                        ALU.mult, ALU.add),
                        waits=[("dWQ", g2v)], inc="sDVE")
                    stg_used[par] = C["sDVE"]
                    P.add("DVE", lambda e, par=par: e.tensor_tensor(
                        ep[par][:], ep[par][:], xo2[par][:], ALU.add),
                        waits=[("dSP", xv)], inc="sDVE")
                    xo_used[par] = C["sDVE"]
                    P.add("DVE", lambda e, par=par: e.tensor_tensor(
                        ep[par][:], ep[par][:], bias_bc[:], ALU.add),
                        waits=[("sDVE", bias_bc_done)], inc="sDVE")
                    ep_used[par] = C["sDVE"]
                    d_fin = C["sDVE"]
                    P.add("SP", dma(y[tt * 128:(tt + 1) * 128, :], ep[par][:]),
                          waits=[("sDVE", d_fin)], inc="dOUT", incn=16)
                    ost_dma[par] = C["dOUT"]

            P.add("SP", lambda e: e.nop(), waits=[("dOUT", C["dOUT"])])

        block = top.enter_context(nc.Block())
        P.emit(block, sems)

    return nc


_NC_CACHE = {}


def _get_nc():
    if "nc" not in _NC_CACHE:
        _NC_CACHE["nc"] = build_program()
    return _NC_CACHE["nc"]


def _host_layout(caw, cab, cpw, cpb, rw, win, wout, fb):
    A1h = np.ascontiguousarray(
        caw[:D].reshape(NHEADS, 128, NDC, 128).transpose(0, 3, 2, 1)
    ).reshape(NHEADS, 128, NDC * 128)
    kv_t = caw[D:].reshape(2, 128, NDC, 128)
    KVi = np.empty((128, NDC * 256), np.float32)
    for dc in range(NDC):
        KVi[:, dc * 256:dc * 256 + 128] = kv_t[0, :, dc, :].T
        KVi[:, dc * 256 + 128:dc * 256 + 256] = kv_t[1, :, dc, :].T
    R1h = np.ascontiguousarray(
        rw.reshape(NE, NDC, 128).transpose(2, 1, 0)).reshape(128, NDC * NE)
    P5h = np.ascontiguousarray(cpw.T).reshape(NDC, 128, 2048)
    W1h = np.ascontiguousarray(
        win.reshape(NE, 2, 32, 128, NDC, 128).transpose(0, 2, 5, 4, 1, 3)
    ).reshape(NE * 32, 128, NDC * 256)
    M4h = np.ascontiguousarray(wout.transpose(0, 2, 1)).reshape(NE * 32, 128, 2048)
    bqh_all = cab[:D].reshape(NHEADS, 128)
    bkh = np.ascontiguousarray(cab[D:D + 128].reshape(128, 1))
    bvh = np.ascontiguousarray(cab[D + 128:].reshape(1, 128))
    bsh = np.ascontiguousarray((cpb + fb).reshape(NDC, 128).T)
    bsrh = np.ascontiguousarray((cpb + fb).reshape(1, 2048))
    trih = np.triu(np.ones((128, 128), np.float32), 1)
    ioth = np.ascontiguousarray(
        np.arange(2048, dtype=np.float32).reshape(16, 128).T)
    io0h = np.arange(128, dtype=np.int32).reshape(128, 1)
    kp = np.arange(128, dtype=np.int64)[:, None]
    tok = np.arange(512, dtype=np.int64)[None, :]
    mskh = np.concatenate(
        [(r * 128 + kp <= tok).astype(np.float32) for r in range(4)], axis=1)
    idnh = np.eye(128, dtype=np.float32)
    return (A1h, KVi, R1h, P5h, W1h, M4h, bqh_all, bkh, bvh, bsh, mskh, idnh,
            bsrh, trih, ioth, io0h)


def kernel(L0_attn_hidden_states, L0_attn_c_attn_w, L0_attn_c_attn_b,
           L0_attn_c_proj_w, L0_attn_c_proj_b, L0_ffn_router_weight,
           L0_ffn_w_in, L0_ffn_w_out, L0_ffn_bias):
    x = np.ascontiguousarray(np.asarray(L0_attn_hidden_states, np.float32))
    caw = np.asarray(L0_attn_c_attn_w, np.float32)
    cab = np.asarray(L0_attn_c_attn_b, np.float32)
    cpw = np.asarray(L0_attn_c_proj_w, np.float32)
    cpb = np.asarray(L0_attn_c_proj_b, np.float32)
    rw = np.asarray(L0_ffn_router_weight, np.float32)
    win = np.asarray(L0_ffn_w_in, np.float32)
    wout = np.asarray(L0_ffn_w_out, np.float32)
    fb = np.asarray(L0_ffn_bias, np.float32)

    (A1h, KVi, R1h, P5h, W1h, M4h, bqh_all, bkh, bvh, bsh, mskh, idnh,
     bsrh, trih, ioth, io0h) = _host_layout(caw, cab, cpw, cpb, rw, win, wout, fb)

    in_maps = []
    for c in range(8):
        b, h = c // 2, c % 2
        xb = x[b]
        xo = np.ascontiguousarray(xb[h * TOWN:(h + 1) * TOWN])
        bqh = np.ascontiguousarray(bqh_all[h * NH:(h + 1) * NH].T)
        hmv = np.zeros((128, 2), np.float32)
        hmv[:, 0] = 1.0 - h
        hmv[:, 1] = float(h)
        in_maps.append({
            "xin": xb, "xown": xo,
            "A1": A1h[h * NH:(h + 1) * NH], "KV1": KVi, "R1": R1h,
            "P5": P5h, "W1": W1h, "M4": M4h,
            "bqd": bqh, "bkd": bkh, "bvd": bvh, "bsd": bsh,
            "mskd": mskh, "idnd": idnh, "hmd": hmv,
            "bsrd": bsrh, "trid": trih, "iotd": ioth, "io0d": io0h,
        })

    nc = _get_nc()
    res = run_bass_kernel_spmd(nc, in_maps, core_ids=list(range(8)))

    out = np.empty((NB, S, D), np.float32)
    for c in range(8):
        b, h = c // 2, c % 2
        out[b, h * TOWN:(h + 1) * TOWN, :] = res.results[c]["y"]
    return out


if __name__ == "__main__":
    import time
    t0 = time.time()
    nc = build_program()
    t1 = time.time()
    n = sum(1 for _ in nc.all_instructions())
    print(f"build {t1 - t0:.1f}s, {n} instructions")


# version 17
# speedup vs baseline: 1.1196x; 1.1196x over previous
import math
from collections import defaultdict
from contextlib import ExitStack

import numpy as np

import concourse.bass as bass
from concourse.bass_core_rust import IndirectOffsetOnAxis
import concourse.mybir as mybir
from concourse.bass_utils import run_bass_kernel_spmd

F32 = mybir.dt.float32
BF16 = mybir.dt.bfloat16
AF = mybir.ActivationFunctionType
ALU = mybir.AluOpType
AX = mybir.AxisListType.X
I32 = mybir.dt.int32

D = 2048
S = 4096
NB = 4
NHEADS = 16
NH = 8
HD = 128
NE = 8
NDC = 16
NSUP = 8
TOWN = 2048
TPASS = 1024
NPASS = 2
ALPHA = 1.0 / math.sqrt(HD)
NEG = -1e38
EPS = 1e-5

ENGS = ("PE", "ACT", "DVE", "POOL", "SP")
DBG = False


class Prog:

    def __init__(self):
        self.ops = {e: [] for e in ENGS}
        self.cnt = defaultdict(int)
        self.bar_val = 0

    def add(self, eng, fn, waits=(), inc=None, incn=1):
        w = [(s, v) for (s, v) in waits if v > 0]
        if self.bar_val > 0:
            w.append(("sBAR", self.bar_val))
        self.ops[eng].append((fn, w, inc, incn))
        if inc is not None:
            self.cnt[inc] += incn
            return self.cnt[inc]
        return None

    def barrier(self):
        save = self.bar_val
        self.bar_val = 0
        for e in ENGS:
            self.add(e, lambda x: x.nop(), waits=[("sBAR", save)],
                     inc="sBAR")
        self.bar_val = self.cnt["sBAR"]

    def emit(self, block, sems):
        section_of = {
            "PE": block.tensor, "ACT": block.scalar, "DVE": block.vector,
            "POOL": block.gpsimd, "SP": block.sync,
        }
        for eng_name in ENGS:
            ops_local = self.ops[eng_name]
            if not ops_local:
                continue

            def make(ops_l):
                def body(eng):
                    known = defaultdict(int)
                    for fn, waits, inc, incn in ops_l:
                        for semname, val in waits:
                            if val > known[semname]:
                                eng.wait_ge(sems[semname], val)
                                known[semname] = val
                        instr = fn(eng)
                        if inc is not None:
                            instr.then_inc(sems[inc], incn)
                return body
            section_of[eng_name](make(ops_local))


def build_program():
    nc = bass.Bass("TRN2", target_bir_lowering=False)

    xin = nc.dram_tensor("xin", [S, D], F32, kind="ExternalInput").ap()
    xown = nc.dram_tensor("xown", [TOWN, D], F32, kind="ExternalInput").ap()
    tri_d = nc.dram_tensor("trid", [128, 128], F32, kind="ExternalInput").ap()
    iot_d = nc.dram_tensor("iotd", [128, 16], F32, kind="ExternalInput").ap()
    io0_d = nc.dram_tensor("io0d", [128, 1], mybir.dt.int32, kind="ExternalInput").ap()
    bsr_d = nc.dram_tensor("bsrd", [1, D], F32, kind="ExternalInput").ap()
    A1 = nc.dram_tensor("A1", [NH, 128, NDC * 128], F32, kind="ExternalInput").ap()
    KV1 = nc.dram_tensor("KV1", [128, NDC * 256], F32, kind="ExternalInput").ap()
    R1 = nc.dram_tensor("R1", [128, NDC * 8], F32, kind="ExternalInput").ap()
    P5 = nc.dram_tensor("P5", [NDC, 128, 2048], F32, kind="ExternalInput").ap()
    W1 = nc.dram_tensor("W1", [NE * 32, 128, NDC * 256], F32, kind="ExternalInput").ap()
    M4 = nc.dram_tensor("M4", [NE * 32, 128, 2048], F32, kind="ExternalInput").ap()
    bq_d = nc.dram_tensor("bqd", [128, NH], F32, kind="ExternalInput").ap()
    bk_d = nc.dram_tensor("bkd", [128, 1], F32, kind="ExternalInput").ap()
    bv_d = nc.dram_tensor("bvd", [1, 128], F32, kind="ExternalInput").ap()
    bs_d = nc.dram_tensor("bsd", [128, NDC], F32, kind="ExternalInput").ap()
    msk_d = nc.dram_tensor("mskd", [128, 2048], F32, kind="ExternalInput").ap()
    idn_d = nc.dram_tensor("idnd", [128, 128], F32, kind="ExternalInput").ap()
    hm_d = nc.dram_tensor("hmd", [128, 2], F32, kind="ExternalInput").ap()
    y = nc.dram_tensor("y", [TOWN, D], F32, kind="ExternalOutput").ap()
    dbg = {}
    if DBG:
        dbg["k"] = nc.dram_tensor("dbg_k", [128, S], BF16, kind="ExternalOutput").ap()
        dbg["v"] = nc.dram_tensor("dbg_v", [128, S], BF16, kind="ExternalOutput").ap()
        dbg["q"] = nc.dram_tensor("dbg_q", [128, S], BF16, kind="ExternalOutput").ap()
        dbg["cco"] = nc.dram_tensor("dbg_cco", [NHEADS * 128, TOWN], BF16, kind="ExternalOutput").ap()
        dbg["gates"] = nc.dram_tensor("dbg_gates", [128, 8 * NE], F32, kind="ExternalOutput").ap()
        dbg["xnp"] = nc.dram_tensor("dbg_xnp", [128, NDC * TPASS], BF16, kind="ExternalOutput").ap()
        dbg["acc"] = nc.dram_tensor("dbg_acc", [128, NDC * TPASS], F32, kind="ExternalOutput").ap()
    CAPD = 768
    xe_d = nc.dram_tensor("xe_dr", [NE * CAPD + 1, D], BF16, kind="Internal").ap()
    o_d = nc.dram_tensor("o_dr", [NE * CAPD + 1, D], F32, kind="Internal").ap()
    acc_d = nc.dram_tensor("acc_dr", [TOWN, D], F32, kind="Internal").ap()
    cc_in = nc.dram_tensor("cc_in", [2 * NHEADS * 128, TOWN], BF16, kind="Internal").ap()
    cc_out = nc.dram_tensor("cc_out", [NHEADS * 128, TOWN], BF16, kind="Internal").ap()

    P = Prog()
    C = P.cnt

    def dma(out, in_):
        return lambda e: e.dma_start(out=out, in_=in_)

    with ExitStack() as top:
        def sbuf(name, shape, dt=F32):
            return top.enter_context(nc.sbuf_tensor(name, shape, dt))

        sems = {}
        for sname in ("dSP", "dWQ", "dOUT", "sPE", "sACT", "sDVE", "sPOOL", "sBAR"):
            sems[sname] = top.enter_context(nc.semaphore(sname))

        ident_b = sbuf("ident_b", [128, 128], BF16)
        ident_f = sbuf("ident_f", [128, 128], F32)
        mask = sbuf("mask", [128, 2048], F32)
        onesr = sbuf("onesr", [1, 128], F32)
        bq = sbuf("bq_s", [128, NH], F32)
        bk = sbuf("bk_s", [128, 1], F32)
        bvrow = sbuf("bvrow", [1, 128], F32)
        bvb = sbuf("bvb", [128, 128], F32)
        bsum = sbuf("bsum_s", [128, NDC], F32)
        hmask = sbuf("hmask", [128, 2], F32)

        P.add("SP", dma(mask[:], msk_d), inc="dSP", incn=16)
        P.add("SP", dma(ident_f[:], idn_d), inc="dSP", incn=16)
        P.add("SP", dma(bq[:], bq_d), inc="dSP", incn=16)
        P.add("SP", dma(bk[:], bk_d), inc="dSP", incn=16)
        P.add("SP", dma(bsum[:], bs_d), inc="dSP", incn=16)
        P.add("SP", dma(bvrow[:], bv_d), inc="dSP", incn=16)
        P.add("SP", dma(hmask[:], hm_d), inc="dSP", incn=16)
        consts_sp = C["dSP"]
        identb_done = P.add("POOL", dma(ident_b[:], idn_d), inc="dWQ", incn=16)
        ones_done = P.add("DVE", lambda e: e.memset(onesr[:], 1.0), inc="sDVE")

        def emit_ln(xsrc, sxbuf, xdisc_ap, xnt, st, war_act, war_dve):
            P.add("SP", dma(sxbuf[:], xsrc),
                  waits=[("sACT", war_act), ("sDVE", war_dve)],
                  inc="dSP", incn=16)
            ldv = C["dSP"]
            P.add("ACT", lambda e: e.activation(
                xdisc_ap, sxbuf[:], AF.Square, accum_out=st["sq"][:]),
                waits=[("dSP", ldv)], inc="sACT")
            P.add("ACT", lambda e: e.activation(
                xdisc_ap, sxbuf[:], AF.Copy, accum_out=st["sum"][:]), inc="sACT")
            a_sum = C["sACT"]
            P.add("DVE", lambda e: e.tensor_scalar(
                st["mean"][:], st["sum"][:], 1.0 / D, None, ALU.mult),
                waits=[("sACT", a_sum)], inc="sDVE")
            d1 = C["sDVE"]
            P.add("DVE", lambda e: e.tensor_tensor(
                st["ms"][:], st["mean"][:], st["mean"][:], ALU.mult),
                waits=[("sDVE", d1)], inc="sDVE")
            d2 = C["sDVE"]
            P.add("DVE", lambda e: e.tensor_scalar(
                st["ve"][:], st["sq"][:], 1.0 / D, EPS, ALU.mult, ALU.add),
                inc="sDVE")
            d3 = C["sDVE"]
            P.add("DVE", lambda e: e.tensor_tensor(
                st["ve"][:], st["ve"][:], st["ms"][:], ALU.subtract),
                waits=[("sDVE", d3)], inc="sDVE")
            d_ve = C["sDVE"]
            P.add("ACT", lambda e: e.activation(st["lnv"][:], st["ve"][:], AF.Ln),
                  waits=[("sDVE", d_ve)], inc="sACT")
            a_ln = C["sACT"]
            P.add("ACT", lambda e: e.activation(
                st["rs"][:], st["lnv"][:], AF.Exp, scale=-0.5),
                waits=[("sACT", a_ln)], inc="sACT")
            a_rs = C["sACT"]
            P.add("DVE", lambda e: e.scalar_tensor_tensor(
                st["mrn"][:], st["mean"][:], -1.0, st["rs"][:],
                ALU.mult, ALU.mult), waits=[("sACT", a_rs)], inc="sDVE")
            d_mrn = C["sDVE"]
            P.add("DVE", lambda e: e.tensor_scalar(
                xnt[:], sxbuf[:], st["rs"][:], st["mrn"][:],
                ALU.mult, ALU.add), waits=[("sDVE", d_mrn)], inc="sDVE")
            return C["sDVE"], a_sum

        def emit_xpose(xnt, pxT, px_evac, dest, dstride, off, xnt_rdy, pe_war):
            for half in range(2):
                px = pxT[half]

                def tp(e, half=half, px=px):
                    ins = None
                    for i in range(8):
                        dc = half * 8 + i
                        ins = e.transpose(px[:, i * 128:(i + 1) * 128],
                                          xnt[:, dc * 128:(dc + 1) * 128],
                                          ident_b[:])
                    return ins
                pe_c = P.add("PE", tp,
                             waits=[("sDVE", xnt_rdy), ("dWQ", identb_done),
                                    ("sDVE", px_evac[half]), ("sPE", pe_war)],
                             inc="sPE")

                def ev(e, half=half, px=px):
                    dv = dest[:].rearrange("p (dc t) -> p dc t", t=dstride)
                    return e.tensor_copy(
                        dv[:, half * 8:(half + 1) * 8, off:off + 128],
                        px[:].rearrange("p (dc t) -> p dc t", t=128))
                P.add("DVE", ev, waits=[("sPE", pe_c)], inc="sDVE")
                px_evac[half] = C["sDVE"]

        Q_dram = nc.dram_tensor("Q_dram", [NH, 128, S], BF16, kind="Internal").ap()

        with ExitStack() as ph12:
            def sb12(name, shape, dt=F32):
                return ph12.enter_context(nc.sbuf_tensor(name, shape, dt))

            K_sb = sb12("K_sb", [128, S], BF16)
            V_sb = sb12("V_sb", [128, S], BF16)

            with ExitStack() as ph1:
                def sb1(name, shape, dt=F32):
                    return ph1.enter_context(nc.sbuf_tensor(name, shape, dt))

                sx = [sb1(f"sx{i}", [128, D]) for i in range(2)]
                xdisc = sb1("xdisc", [128, D], BF16)
                xnt = sb1("xnt", [128, D], BF16)
                xnf = [sb1(f"xnf{i}", [128, NDC * 512], BF16) for i in range(2)]
                wkv = sb1("wkv", [128, NDC * 256], BF16)
                wq8 = sb1("wq8", [128, NH * NDC * 128], BF16)
                qstg = [sb1(f"qstg{i}", [128, NH * 512], BF16) for i in range(2)]
                st1 = {n: sb1("s1_" + n, [128, 1]) for n in
                       ("sum", "sq", "mean", "ms", "ve", "rs", "mrn", "lnv")}

                pxT = [ph1.enter_context(
                    nc.psum_tensor(f"pxT{i}", [128, 1024], BF16)) for i in range(2)]
                pq = [ph1.enter_context(
                    nc.psum_tensor(f"pq{i}", [128, 512], F32)) for i in range(2)]
                pk = ph1.enter_context(nc.psum_tensor("pk", [128, 512], F32))
                pv = ph1.enter_context(nc.psum_tensor("pv", [128, 512], F32))

                kv_done = P.add("POOL", dma(wkv[:], KV1), inc="dWQ", incn=16)
                wq_done = P.add("POOL", dma(
                    wq8[:].rearrange("p (h c) -> p h c", h=NH),
                    A1.rearrange("h p c -> p h c")), inc="dWQ", incn=16)

                mm0 = P.add("PE", lambda e: e.matmul(
                    pk[:, 0:128], onesr[:], bvrow[:], start=True, stop=True),
                    waits=[("dSP", consts_sp), ("sDVE", ones_done)], inc="sPE")
                bvb_done = P.add("DVE", lambda e: e.tensor_copy(bvb[:], pk[:, 0:128]),
                                 waits=[("sPE", mm0)], inc="sDVE")

                px_evac = [0, 0]
                sx_used_a = [0, 0]
                sx_used_d = [0, 0]
                xnf_pe_used = [0, 0]
                k_evac = 0
                v_evac = 0
                q_evac = [0, 0]
                qs_dma = [0, 0]

                for sup in range(NSUP):
                    par = sup % 2
                    xs = xnf[par]
                    for tt in range(4):
                        t = sup * 4 + tt
                        rdy, asum = emit_ln(xin[t * 128:(t + 1) * 128, :],
                                            sx[t % 2], xdisc[:], xnt, st1,
                                            sx_used_a[t % 2], sx_used_d[t % 2])
                        sx_used_a[t % 2] = asum
                        sx_used_d[t % 2] = rdy
                        emit_xpose(xnt, pxT, px_evac, xs, 512, tt * 128, rdy,
                                   xnf_pe_used[par])
                    xnf_rdy = C["sDVE"]

                    def kmm(e, xs=xs):
                        ins = None
                        for dc in range(NDC):
                            ins = e.matmul(pk[:, 0:512],
                                           wkv[:, dc * 256:dc * 256 + 128],
                                           xs[:, dc * 512:(dc + 1) * 512],
                                           start=(dc == 0), stop=(dc == NDC - 1))
                        return ins
                    pe_k = P.add("PE", kmm,
                                 waits=[("sDVE", xnf_rdy), ("dWQ", kv_done),
                                        ("sDVE", k_evac), ("sDVE", bvb_done)],
                                 inc="sPE")
                    P.add("DVE", lambda e, sup=sup: e.tensor_scalar(
                        K_sb[:, sup * 512:(sup + 1) * 512], pk[:, 0:512],
                        bk[:], None, ALU.add),
                        waits=[("sPE", pe_k)], inc="sDVE")
                    k_evac = C["sDVE"]

                    def vmm(e, xs=xs):
                        ins = None
                        for tt in range(4):
                            for dc in range(NDC):
                                ins = e.matmul(
                                    pv[:, tt * 128:(tt + 1) * 128],
                                    xs[:, dc * 512 + tt * 128:dc * 512 + (tt + 1) * 128],
                                    wkv[:, dc * 256 + 128:dc * 256 + 256],
                                    start=(dc == 0), stop=(dc == NDC - 1))
                        return ins
                    pe_v = P.add("PE", vmm, waits=[("sDVE", v_evac)], inc="sPE")
                    for tt in range(4):
                        P.add("DVE", lambda e, sup=sup, tt=tt: e.tensor_tensor(
                            V_sb[:, (sup * 4 + tt) * 128:(sup * 4 + tt + 1) * 128],
                            pv[:, tt * 128:(tt + 1) * 128], bvb[:], ALU.add),
                            waits=[("sPE", pe_v)], inc="sDVE")
                    v_evac = C["sDVE"]

                    for hh in range(NH):
                        bank = pq[hh % 2]

                        def qmm(e, hh=hh, bank=bank, xs=xs):
                            ins = None
                            for dc in range(NDC):
                                ins = e.matmul(
                                    bank[:, 0:512],
                                    wq8[:, (hh * NDC + dc) * 128:(hh * NDC + dc + 1) * 128],
                                    xs[:, dc * 512:(dc + 1) * 512],
                                    start=(dc == 0), stop=(dc == NDC - 1))
                            return ins
                        pe_q = P.add("PE", qmm,
                                     waits=[("dWQ", wq_done),
                                            ("sDVE", q_evac[hh % 2])], inc="sPE")
                        P.add("DVE", lambda e, hh=hh, bank=bank, par=par: e.tensor_scalar(
                            qstg[par][:, hh * 512:(hh + 1) * 512],
                            bank[:, 0:512], bq[:, hh:hh + 1], None, ALU.add),
                            waits=[("sPE", pe_q), ("dSP", qs_dma[par])],
                            inc="sDVE")
                        q_evac[hh % 2] = C["sDVE"]
                    q_stage_rdy = C["sDVE"]
                    P.add("SP", dma(
                        Q_dram.rearrange("h p c -> p h c")[:, :, sup * 512:(sup + 1) * 512],
                        qstg[par][:].rearrange("p (h c) -> p h c", c=512)),
                        waits=[("sDVE", q_stage_rdy)], inc="dSP", incn=16)
                    qs_dma[par] = C["dSP"]
                    xnf_pe_used[par] = C["sPE"]

                kq_ready = C["sDVE"]
                q_dram_done = C["dSP"]
                if DBG:
                    P.add("SP", dma(dbg["k"], K_sb[:]), waits=[("sDVE", kq_ready)],
                          inc="dOUT", incn=16)
                    P.add("SP", dma(dbg["v"], V_sb[:]), waits=[("sDVE", kq_ready)],
                          inc="dOUT", incn=16)
                    P.add("SP", dma(dbg["q"], Q_dram[0]), waits=[("dSP", q_dram_done)],
                          inc="dOUT", incn=16)

            with ExitStack() as ph2:
                def sb2(name, shape, dt=F32):
                    return ph2.enter_context(nc.sbuf_tensor(name, shape, dt))

                Pbuf = sb2("Pbuf", [128, 2 * 32 * 512], BF16)
                Q_h = [sb2(f"Qh{i}", [128, S], BF16) for i in range(2)]
                ctx_h = [sb2(f"ctx{i}", [128, S], BF16) for i in range(2)]
                cst = [sb2(f"cst{i}", [128, S], BF16) for i in range(2)]
                ones128 = sb2("ones128", [128, 128], BF16)
                rrb = sb2("rrb", [128, 512])

                ps_s = [ph2.enter_context(
                    nc.psum_tensor(f"ps_s{i}", [128, 512], F32)) for i in range(4)]
                ps_c = [ph2.enter_context(
                    nc.psum_tensor(f"ps_c{i}", [128, 512], F32)) for i in range(2)]
                rr_ps = [ph2.enter_context(
                    nc.psum_tensor(f"rr_ps{i}", [128, 512], F32)) for i in range(2)]

                ones_dn = P.add("DVE", lambda e: e.memset(ones128[:], 1.0),
                                inc="sDVE")
                ctx_evac = [0, 0]
                cst_dma = [0, 0]
                p_used = [0, 0]
                q_dma = [0, 0]
                q_used = [0, 0]

                def emit_qload(h):
                    qp = h % 2
                    P.add("SP", dma(Q_h[qp][:], Q_dram[h]),
                          waits=[("sPE", q_used[qp]), ("dSP", q_dram_done)],
                          inc="dSP", incn=16)
                    q_dma[qp] = C["dSP"]

                def emit_cst(hh, ctxh_done):
                    hp = hh % 2
                    P.add("DVE", lambda e, hp=hp: e.tensor_scalar(
                        cst[hp][:], ctx_h[hp][:], hmask[:, 0:1], None, ALU.mult),
                        waits=[("sDVE", ctxh_done), ("dSP", cst_dma[hp])],
                        inc="sDVE")
                    d_c0 = C["sDVE"]
                    for d in range(2):
                        P.add("SP", dma(
                            cc_in[(d * NHEADS + hh) * 128:(d * NHEADS + hh + 1) * 128, :],
                            cst[hp][:, d * TOWN:(d + 1) * TOWN]),
                            waits=[("sDVE", d_c0)], inc="dSP", incn=16)
                    c0_dma = C["dSP"]
                    P.add("DVE", lambda e, hp=hp: e.tensor_scalar(
                        cst[hp][:], ctx_h[hp][:], hmask[:, 1:2], None, ALU.mult),
                        waits=[("dSP", c0_dma)], inc="sDVE")
                    d_c1 = C["sDVE"]
                    for d in range(2):
                        P.add("SP", dma(
                            cc_in[(d * NHEADS + NH + hh) * 128:(d * NHEADS + NH + hh + 1) * 128, :],
                            cst[hp][:, d * TOWN:(d + 1) * TOWN]),
                            waits=[("sDVE", d_c1)], inc="dSP", incn=16)
                    cst_dma[hp] = C["dSP"]

                pend = None
                mask_done = 0
                rrb_done = [0, 0]

                def emit_iter(i, hh, g):
                    nonlocal pend, mask_done
                    cur = i is not None
                    if cur:
                        par = i % 2
                        qp = hh % 2
                        kcs = 4 * (g + 1)
                        pb = par * 16384
                    else:
                        kcs = 0
                    prev_exp_last = pend[4] if pend else 0

                    pairs = []
                    if pend is not None:
                        pi_, ph_, pg_, pexp, pel = pend
                        ppb = (pi_ % 2) * 16384
                        pkcs = 4 * (pg_ + 1)
                        cpar = pi_ % 2
                        for kc in range(pkcs):
                            first = (kc == 0)
                            last = (kc == pkcs - 1)

                            def rs_mm(e, kc=kc, ppb=ppb, cpar=cpar,
                                      first=first, last=last):
                                return e.matmul(
                                    rr_ps[cpar][:, 0:512], ones128[:],
                                    Pbuf[:, ppb + kc * 512:ppb + (kc + 1) * 512],
                                    start=first, stop=last)

                            def pv_mm(e, kc=kc, ppb=ppb, cpar=cpar,
                                      first=first, last=last):
                                return e.matmul(
                                    ps_c[cpar][:, 0:512],
                                    V_sb[:, kc * 128:(kc + 1) * 128],
                                    Pbuf[:, ppb + kc * 512:ppb + (kc + 1) * 512],
                                    start=first, stop=last)
                            w1, w2 = [], []
                            if first:
                                w1 = [("sACT", pel), ("sDVE", mask_done),
                                      ("sDVE", rrb_done[cpar]),
                                      ("sDVE", ones_dn)]
                                w2 = [("sDVE", ctx_evac[cpar])]
                            pairs.append((rs_mm, w1, pv_mm, w2))

                    def emit_pair(pr):
                        rs_mm, w1, pv_mm, w2 = pr
                        P.add("PE", rs_mm, waits=w1, inc="sPE")
                        P.add("PE", pv_mm, waits=w2, inc="sPE")

                    exp_at = []

                    def emit_smm(kc):
                        def smm(e, kc=kc, qp=qp, g=g):
                            return e.matmul(
                                ps_s[kc % 4][:, 0:512],
                                K_sb[:, kc * 128:(kc + 1) * 128],
                                Q_h[qp][:, g * 512:(g + 1) * 512],
                                start=True, stop=True)
                        w = []
                        if kc == 0:
                            w = [("sACT", prev_exp_last), ("sDVE", kq_ready),
                                 ("dSP", q_dma[qp])]
                        elif kc >= 4:
                            w = [("sACT", exp_at[kc - 4])]
                        pe_s = P.add("PE", smm, waits=w, inc="sPE")
                        ew = [("sPE", pe_s)]
                        if kc == 0:
                            ew.append(("sPE", p_used[par]))
                        P.add("ACT", lambda e, kc=kc, pb=pb: e.activation(
                            Pbuf[:, pb + kc * 512:pb + (kc + 1) * 512],
                            ps_s[kc % 4][:, 0:512], AF.Exp, scale=ALPHA),
                            waits=ew, inc="sACT")
                        exp_at.append(C["sACT"])

                    pidx = 0
                    for kc in range(min(4, kcs)):
                        emit_smm(kc)
                    for kc in range(4, kcs):
                        if pidx < len(pairs):
                            emit_pair(pairs[pidx])
                            pidx += 1
                        emit_smm(kc)
                    while pidx < len(pairs):
                        emit_pair(pairs[pidx])
                        pidx += 1

                    exp_last = C["sACT"] if cur else prev_exp_last
                    if cur and g == NSUP - 1:
                        q_used[qp] = C["sPE"]

                    if pend is not None:
                        pi_, ph_, pg_, pexp, pel = pend
                        cpar = pi_ % 2
                        hp = ph_ % 2
                        pv_last = C["sPE"]
                        p_used[cpar] = pv_last
                        P.add("DVE", lambda e, cpar=cpar: e.reciprocal(
                            rrb[:], rr_ps[cpar][:, 0:512]),
                            waits=[("sPE", pv_last)], inc="sDVE")
                        rrb_done[cpar] = C["sDVE"]
                        P.add("DVE", lambda e, pg_=pg_, cpar=cpar, hp=hp: e.tensor_tensor(
                            ctx_h[hp][:, pg_ * 512:(pg_ + 1) * 512],
                            ps_c[cpar][:, 0:512], rrb[:], ALU.mult),
                            waits=[("sDVE", rrb_done[cpar])], inc="sDVE")
                        ctx_evac[cpar] = C["sDVE"]
                        if pg_ == NSUP - 1:
                            emit_cst(ph_, ctx_evac[cpar])

                    if cur:
                        P.add("DVE", lambda e, pb=pb, g=g: e.tensor_tensor(
                            Pbuf[:, pb + 4 * g * 512:pb + 4 * g * 512 + 2048],
                            Pbuf[:, pb + 4 * g * 512:pb + 4 * g * 512 + 2048],
                            mask[:], ALU.mult),
                            waits=[("sACT", exp_last), ("dSP", consts_sp)],
                            inc="sDVE")
                        mask_done = C["sDVE"]
                        pend = (i, hh, g, exp_at, exp_last)
                    else:
                        pend = None

                emit_qload(0)
                if NH > 1:
                    emit_qload(1)
                flat = [(h, g) for h in range(NH) for g in range(NSUP)]
                for i, (hh, g) in enumerate(flat):
                    if g == 2 and 2 <= hh + 1 < NH:
                        emit_qload(hh + 1)
                    emit_iter(i, hh, g)
                emit_iter(None, None, None)
                ctx_writes = C["dSP"]

        def coll(e):
            return e.collective_compute(
                "ReduceScatter", ALU.add,
                replica_groups=[[0, 1], [2, 3], [4, 5], [6, 7]],
                ins=[cc_in.opt()], outs=[cc_out.opt()])
        P.add("POOL", coll, waits=[("dSP", ctx_writes)], inc="sPOOL")
        cc_done = C["sPOOL"]
        if DBG:
            P.add("POOL", dma(dbg["cco"], cc_out), waits=[("sPOOL", cc_done)],
                  inc="dOUT", incn=16)

        P.barrier()

        CAP = 768
        NST = CAP // 128
        with ExitStack() as ph3:
            def sb3(name, shape, dt=F32):
                return ph3.enter_context(nc.sbuf_tensor(name, shape, dt))

            sxo = [sb3(f"sxo{i}", [128, D]) for i in range(2)]
            xnf_t = sb3("xnf_t", [128, D], BF16)
            gates_p = sb3("gates_p", [128, 16 * NE], F32)
            lg = sb3("lg", [128, 16 * NE], F32)
            m8 = sb3("m8", [128, 16 * NE], F32)
            e1m = sb3("e1m", [128, NE], F32)
            e2m = sb3("e2m", [128, NE], F32)
            gd = sb3("gd", [128, 16], F32)
            g1 = sb3("g1", [128, 16], F32)
            g2 = sb3("g2", [128, 16], F32)
            gsig = sb3("gsig", [128, 16], F32)
            r1s3 = sb3("r1s3", [128, NDC * 8], BF16)
            st3 = {n: sb3("s3_" + n, [128, 1]) for n in
                   ("sum", "sq", "mean", "ms", "ve", "rs", "mrn", "lnv")}
            mskb = sb3("mskb", [128, 16], F32)
            zz16 = sb3("zz16", [128, 16], F32)
            csum = sb3("csum", [128, 16], F32)
            rcnt = sb3("rcnt", [128, 1], F32)
            baseb = sb3("baseb", [128, 1], F32)
            slotb = sb3("slotb", [128, 16], F32)
            posf = sb3("posf", [128, 16], F32)
            d1f = sb3("d1f", [128, 16], F32)
            d2f = sb3("d2f", [128, 16], F32)
            destt = sb3("destt", [128, 16], F32)
            dtmp = sb3("dtmp", [128, 16], F32)
            over = sb3("over", [128, 16], F32)
            d1i = sb3("d1i", [128, 16], I32)
            scn8 = sb3("scn8", [128, 8], F32)
            u8 = sb3("u8", [128, 8], F32)
            kcol = sb3("kcol", [128, 1], F32)
            d2i = sb3("d2i", [128, 16], I32)
            e1all = sb3("e1all", [128, 16 * NE], F32)
            e2all = sb3("e2all", [128, 16 * NE], F32)
            zbf = sb3("zbf", [128, D], BF16)
            brow = sb3("brow", [1, D], F32)
            zf32 = sb3("zf32", [1, D], F32)
            pstg = [sb3(f"pstg{i}", [128, 512], F32) for i in range(2)]

            r1_done = P.add("POOL", dma(r1s3[:], R1), inc="dWQ", incn=16)
            idx_consts = P.add("SP", dma(brow[:], bsr_d), inc="dSP", incn=16)

            with ExitStack() as pro:
                xnTM = pro.enter_context(
                    nc.sbuf_tensor("xnTM", [128, 16 * D], BF16))
                tri_sb = pro.enter_context(nc.sbuf_tensor("tri_sb", [128, 128], F32))
                P.add("SP", dma(tri_sb[:], tri_d), inc="dSP", incn=16)
                idx_consts2 = C["dSP"]
                pxT3 = [pro.enter_context(nc.psum_tensor(
                    f"px3_{i}", [128, 1024], BF16)) for i in range(2)]
                plg = pro.enter_context(nc.psum_tensor("plg", [128, 512], F32))

                px_evac3 = [0, 0]
                sxo_a = [0, 0]
                sxo_d = [0, 0]
                lg_evac = 0
                rt_pe = 0
                for tt in range(16):
                    spar = tt % 2
                    xslice = xnTM[:, tt * D:(tt + 1) * D]
                    rdy, asum = emit_ln(xown[tt * 128:(tt + 1) * 128, :],
                                        sxo[spar], zbf[:], xslice, st3,
                                        sxo_a[spar], sxo_d[spar])
                    sxo_a[spar] = asum
                    sxo_d[spar] = rdy
                    for half in range(2):
                        px = pxT3[half]

                        def tp3(e, half=half, px=px, xslice=xslice):
                            ins = None
                            for i in range(8):
                                dc = half * 8 + i
                                ins = e.transpose(
                                    px[:, i * 128:(i + 1) * 128],
                                    xslice[:, dc * 128:(dc + 1) * 128],
                                    ident_b[:])
                            return ins
                        pe_c = P.add("PE", tp3,
                                     waits=[("sDVE", rdy), ("sDVE", px_evac3[half]),
                                            ("dWQ", identb_done)], inc="sPE")

                        def ev3(e, half=half, px=px):
                            dv = xnf_t[:].rearrange("p (dc t) -> p dc t", t=128)
                            return e.tensor_copy(
                                dv[:, half * 8:(half + 1) * 8, :],
                                px[:].rearrange("p (dc t) -> p dc t", t=128))
                        P.add("DVE", ev3, waits=[("sPE", pe_c), ("sPE", rt_pe)],
                              inc="sDVE")
                        px_evac3[half] = C["sDVE"]
                    xt_rdy = C["sDVE"]

                    def rmm(e):
                        ins = None
                        for dc in range(NDC):
                            ins = e.matmul(
                                plg[:, 0:8],
                                xnf_t[:, dc * 128:(dc + 1) * 128],
                                r1s3[:, dc * 8:(dc + 1) * 8],
                                start=(dc == 0), stop=(dc == NDC - 1))
                        return ins
                    pe_r = P.add("PE", rmm,
                                 waits=[("sDVE", xt_rdy), ("dWQ", r1_done),
                                        ("sDVE", lg_evac)], inc="sPE")
                    rt_pe = pe_r
                    P.add("DVE", lambda e, tt=tt: e.tensor_copy(
                        lg[:, tt * 8:(tt + 1) * 8], plg[:, 0:8]),
                        waits=[("sPE", pe_r)], inc="sDVE")
                    lg_evac = C["sDVE"]
                xnTM_rdy = C["sDVE"]
                P.add("DVE", lambda e: e.memset(zz16[:], 0.0), inc="sDVE")

                for tt in range(16):
                    P.add("DVE", lambda e, tt=tt: e.max(
                        m8[:, tt * 8:(tt + 1) * 8], lg[:, tt * 8:(tt + 1) * 8]),
                        waits=[("sDVE", xnTM_rdy)], inc="sDVE")
                d_m8 = C["sDVE"]
                m8v = m8[:].rearrange("p (t e) -> p t e", e=8)
                P.add("DVE", lambda e: e.tensor_tensor(
                    gd[:], m8v[:, :, 0], m8v[:, :, 1], ALU.subtract),
                    waits=[("sDVE", d_m8)], inc="sDVE")
                d_gd = C["sDVE"]
                P.add("ACT", lambda e: e.activation(
                    gsig[:], gd[:], AF.Exp, scale=-1.0),
                    waits=[("sDVE", d_gd)], inc="sACT")
                a_gs = C["sACT"]
                P.add("DVE", lambda e: e.tensor_scalar(
                    gsig[:], gsig[:], 1.0, None, ALU.add),
                    waits=[("sACT", a_gs)], inc="sDVE")
                d_g1p = C["sDVE"]
                P.add("DVE", lambda e: e.reciprocal(g1[:], gsig[:]),
                      waits=[("sDVE", d_g1p)], inc="sDVE")
                d_g1 = C["sDVE"]
                P.add("DVE", lambda e: e.tensor_scalar(
                    g2[:], g1[:], -1.0, 1.0, ALU.mult, ALU.add),
                    waits=[("sDVE", d_g1)], inc="sDVE")
                d_g2 = C["sDVE"]
                P.add("DVE", lambda e: e.nop(), waits=[("sDVE", d_g2)])
                for tt in range(16):
                    sl = (tt * 8, (tt + 1) * 8)
                    P.add("DVE", lambda e, sl=sl, tt=tt: e.tensor_scalar(
                        e1m[:], lg[:, sl[0]:sl[1]],
                        m8[:, tt * 8:tt * 8 + 1], None, ALU.is_equal),
                        inc="sDVE")
                    d0 = C["sDVE"]
                    P.add("DVE", lambda e: e.tensor_tensor_scan(
                        scn8[:], e1m[:], zz16[:, 0:8], 0.0, ALU.add, ALU.add),
                        waits=[("sDVE", d0)], inc="sDVE")
                    d1_ = C["sDVE"]
                    P.add("DVE", lambda e: e.tensor_scalar(
                        u8[:], scn8[:], 1.0, None, ALU.is_equal),
                        waits=[("sDVE", d1_)], inc="sDVE")
                    d2_ = C["sDVE"]
                    P.add("DVE", lambda e, sl=sl: e.tensor_tensor(
                        e1all[:, sl[0]:sl[1]], e1m[:], u8[:], ALU.mult),
                        waits=[("sDVE", d2_)], inc="sDVE")
                    P.add("DVE", lambda e, sl=sl, tt=tt: e.tensor_scalar(
                        e2m[:], lg[:, sl[0]:sl[1]],
                        m8[:, tt * 8 + 1:tt * 8 + 2], None, ALU.is_equal),
                        inc="sDVE")
                    d3_ = C["sDVE"]
                    P.add("DVE", lambda e: e.tensor_tensor_scan(
                        scn8[:], e2m[:], zz16[:, 0:8], 0.0, ALU.add, ALU.add),
                        waits=[("sDVE", d3_)], inc="sDVE")
                    P.add("DVE", lambda e, tt=tt: e.tensor_tensor(
                        kcol[:], m8[:, tt * 8:tt * 8 + 1],
                        m8[:, tt * 8 + 1:tt * 8 + 2], ALU.is_equal),
                        inc="sDVE")
                    d4_ = C["sDVE"]
                    P.add("DVE", lambda e: e.tensor_scalar(
                        kcol[:], kcol[:], 1.0, None, ALU.add),
                        waits=[("sDVE", d4_)], inc="sDVE")
                    d5_ = C["sDVE"]
                    P.add("DVE", lambda e: e.tensor_scalar(
                        u8[:], scn8[:], kcol[:], None, ALU.is_equal),
                        waits=[("sDVE", d5_)], inc="sDVE")
                    d6_ = C["sDVE"]
                    P.add("DVE", lambda e, sl=sl: e.tensor_tensor(
                        e2all[:, sl[0]:sl[1]], e2m[:], u8[:], ALU.mult),
                        waits=[("sDVE", d6_)], inc="sDVE")
                    d_e = C["sDVE"]
                    P.add("DVE", lambda e, sl=sl, tt=tt: e.scalar_tensor_tensor(
                        e1m[:], e1all[:, sl[0]:sl[1]], g1[:, tt:tt + 1],
                        zz16[:, 0:8], ALU.mult, ALU.add),
                        waits=[("sDVE", d_e)], inc="sDVE")
                    d_e1 = C["sDVE"]
                    P.add("DVE", lambda e, sl=sl, tt=tt: e.scalar_tensor_tensor(
                        gates_p[:, sl[0]:sl[1]],
                        e2all[:, sl[0]:sl[1]], g2[:, tt:tt + 1],
                        e1m[:], ALU.mult, ALU.add),
                        waits=[("sDVE", d_e1)], inc="sDVE")
                gates_rdy = C["sDVE"]

                P.add("DVE", lambda e: e.memset(zbf[:], 0.0),
                      waits=[("sACT", C["sACT"])], inc="sDVE")
                P.add("DVE", lambda e: e.memset(zf32[:], 0.0), inc="sDVE")
                z_rdy = C["sDVE"]
                for e_i in range(NE):
                    for st_ in range(NST):
                        P.add("POOL", dma(
                            xe_d[e_i * CAP + st_ * 128:e_i * CAP + (st_ + 1) * 128, :],
                            zbf[:]), waits=[("sDVE", z_rdy)], inc="dWQ", incn=16)
                P.add("POOL", dma(o_d[NE * CAP:NE * CAP + 1, :], zf32[:]),
                      waits=[("sDVE", z_rdy)], inc="dWQ", incn=16)
                zf_done = C["dWQ"]

                TRASH = float(NE * CAP)
                P.add("DVE", lambda e: e.memset(d1f[:], 0.0),
                      waits=[("sDVE", gates_rdy)], inc="sDVE")
                P.add("DVE", lambda e: e.memset(d2f[:], 0.0), inc="sDVE")
                e1v = e1all[:].rearrange("p (t e) -> p t e", e=8)
                e2v = e2all[:].rearrange("p (t e) -> p t e", e=8)
                gpv = gates_p[:].rearrange("p (t e) -> p t e", e=8)
                for e_i in range(NE):
                    P.add("DVE", lambda e, e_i=e_i: e.tensor_scalar(
                        mskb[:], gpv[:, :, e_i], 0.0, None, ALU.is_gt),
                        waits=[("sDVE", gates_rdy)], inc="sDVE")
                    d_mk = C["sDVE"]
                    P.add("DVE", lambda e: e.tensor_tensor_scan(
                        csum[:], mskb[:], zz16[:], 0.0, ALU.add, ALU.add),
                        waits=[("sDVE", d_mk)], inc="sDVE")
                    d_cs = C["sDVE"]
                    P.add("DVE", lambda e: e.tensor_copy(
                        rcnt[:], csum[:, 15:16]),
                        waits=[("sDVE", d_cs)], inc="sDVE")
                    d_rc = C["sDVE"]
                    pe_tr = P.add("PE", lambda e: e.matmul(
                        plg[:, 0:1], tri_sb[:], rcnt[:], start=True, stop=True),
                        waits=[("sDVE", d_rc), ("dSP", idx_consts2)], inc="sPE")
                    P.add("ACT", lambda e: e.copy(baseb[:], plg[:, 0:1]),
                          waits=[("sPE", pe_tr)], inc="sACT")
                    a_bs = C["sACT"]
                    P.add("DVE", lambda e: e.tensor_tensor(
                        slotb[:], csum[:], mskb[:], ALU.subtract),
                        waits=[("sACT", a_bs)], inc="sDVE")
                    d_sl = C["sDVE"]
                    P.add("DVE", lambda e, e_i=e_i: e.tensor_scalar(
                        slotb[:], slotb[:], baseb[:], float(e_i * CAP),
                        ALU.add, ALU.add),
                        waits=[("sDVE", d_sl)], inc="sDVE")
                    d_s2 = C["sDVE"]
                    P.add("DVE", lambda e, e_i=e_i: e.tensor_scalar(
                        over[:], slotb[:], float((e_i + 1) * CAP), None,
                        ALU.is_ge), waits=[("sDVE", d_s2)], inc="sDVE")
                    P.add("DVE", lambda e: e.tensor_scalar(
                        dtmp[:], slotb[:], -1.0, TRASH, ALU.mult, ALU.add),
                        inc="sDVE")
                    d_s3 = C["sDVE"]
                    P.add("DVE", lambda e: e.tensor_tensor(
                        dtmp[:], dtmp[:], over[:], ALU.mult),
                        waits=[("sDVE", d_s3)], inc="sDVE")
                    d_s4 = C["sDVE"]
                    P.add("DVE", lambda e: e.tensor_tensor(
                        destt[:], slotb[:], dtmp[:], ALU.add),
                        waits=[("sDVE", d_s4)], inc="sDVE")
                    d_de0 = C["sDVE"]
                    P.add("DVE", lambda e: e.tensor_scalar(
                        dtmp[:], destt[:], -1.0, TRASH, ALU.mult, ALU.add),
                        waits=[("sDVE", d_de0)], inc="sDVE")
                    P.add("DVE", lambda e: e.tensor_scalar(
                        over[:], mskb[:], -1.0, 1.0, ALU.mult, ALU.add),
                        inc="sDVE")
                    d_de1 = C["sDVE"]
                    P.add("DVE", lambda e: e.tensor_tensor(
                        dtmp[:], dtmp[:], over[:], ALU.mult),
                        waits=[("sDVE", d_de1)], inc="sDVE")
                    d_de2 = C["sDVE"]
                    P.add("DVE", lambda e: e.tensor_tensor(
                        destt[:], destt[:], dtmp[:], ALU.add),
                        waits=[("sDVE", d_de2)], inc="sDVE")
                    d_de = C["sDVE"]
                    P.add("DVE", lambda e, e_i=e_i: e.tensor_tensor(
                        dtmp[:], destt[:], e1v[:, :, e_i], ALU.mult),
                        waits=[("sDVE", d_de)], inc="sDVE")
                    d_p1 = C["sDVE"]
                    P.add("DVE", lambda e: e.tensor_tensor(
                        d1f[:], d1f[:], dtmp[:], ALU.add),
                        waits=[("sDVE", d_p1)], inc="sDVE")
                    P.add("DVE", lambda e, e_i=e_i: e.tensor_tensor(
                        dtmp[:], destt[:], e2v[:, :, e_i], ALU.mult),
                        inc="sDVE")
                    d_p2 = C["sDVE"]
                    P.add("DVE", lambda e: e.tensor_tensor(
                        d2f[:], d2f[:], dtmp[:], ALU.add),
                        waits=[("sDVE", d_p2)], inc="sDVE")
                d_dd = C["sDVE"]
                P.add("DVE", lambda e: e.tensor_copy(d1i[:], d1f[:]),
                      waits=[("sDVE", d_dd)], inc="sDVE")
                P.add("DVE", lambda e: e.tensor_copy(d2i[:], d2f[:]),
                      inc="sDVE")
                posi_rdy = C["sDVE"]
                P.add("DVE", lambda e: e.nop(), waits=[("sDVE", posi_rdy)],
                      inc="sDVE")
                posi_rdy = C["sDVE"]

                for tt in range(16):
                    for dcol in (d1i, d2i):
                        P.add("POOL", lambda e, tt=tt, dcol=dcol: e.indirect_dma_start(
                            out=xe_d,
                            out_offset=IndirectOffsetOnAxis(dcol[:, tt:tt + 1], 0),
                            in_=xnTM[:, tt * D:(tt + 1) * D], in_offset=None,
                            bounds_check=None, oob_is_err=False),
                            waits=[("sDVE", posi_rdy), ("dWQ", zf_done)],
                            inc="dWQ", incn=16)
                scat_done = C["dWQ"]

            act_pro = C["sACT"]
            dve_pro = C["sDVE"]
            with ExitStack() as prj:
                ctxh = prj.enter_context(
                    nc.sbuf_tensor("ctxh", [128, NDC * 1024], BF16))
                p5b = [prj.enter_context(nc.psum_tensor(
                    f"p5p{i}", [128, 512], F32)) for i in range(0)]
                p5s = [prj.enter_context(nc.sbuf_tensor(
                    f"p5s{i}", [128, NDC * 512], BF16)) for i in range(2)]
                po_p = [prj.enter_context(nc.psum_tensor(
                    f"po_p{i}", [128, 512], F32)) for i in range(2)]

                ctx_ld = 0
                p5_used = [0, 0]
                po_free = [0, 0]
                pstg_used = [0, 0]
                acc_w = 0
                for th in range(2):
                    for dc in range(NDC):
                        P.add("SP", dma(
                            ctxh[:, dc * 1024:(dc + 1) * 1024],
                            cc_out[dc * 128:(dc + 1) * 128,
                                   th * 1024:(th + 1) * 1024]),
                            waits=[("sPOOL", cc_done), ("dWQ", scat_done),
                                   ("sPE", C["sPE"] if th == 0 else 0)],
                            inc="dSP", incn=16)
                    ctx_ld = C["dSP"]
                    for dg in range(4):
                        wpar = dg % 2
                        P.add("POOL", dma(
                            p5s[wpar][:].rearrange("p (dc c) -> p dc c", c=512),
                            P5[:, :, dg * 512:(dg + 1) * 512].rearrange(
                                "dc p c -> p dc c")),
                            waits=[("sPE", p5_used[wpar]),
                                   ("dWQ", scat_done)], inc="dWQ", incn=16)
                        wv = C["dWQ"]
                        for t8 in range(8):
                            opar = t8 % 2

                            def pmm(e, th=th, dg=dg, t8=t8, wpar=wpar, opar=opar):
                                ins = None
                                for dc in range(NDC):
                                    ins = e.matmul(
                                        po_p[opar][:, 0:512],
                                        ctxh[:, dc * 1024 + t8 * 128:dc * 1024 + (t8 + 1) * 128],
                                        p5s[wpar][:, dc * 512:(dc + 1) * 512],
                                        start=(dc == 0), stop=(dc == NDC - 1))
                                return ins
                            pe_p = P.add("PE", pmm,
                                         waits=[("dWQ", wv), ("dSP", ctx_ld),
                                                ("sACT", act_pro),
                                                ("sDVE", dve_pro),
                                                ("sDVE", po_free[opar])],
                                         inc="sPE")
                            P.add("DVE", lambda e, opar=opar: e.tensor_copy(
                                pstg[opar][:], po_p[opar][:, 0:512]),
                                waits=[("sPE", pe_p), ("dSP", pstg_used[opar])],
                                inc="sDVE")
                            po_free[opar] = C["sDVE"]
                            d_ev = C["sDVE"]
                            P.add("SP", lambda e, th=th, dg=dg, t8=t8, opar=opar: e.dma_start(
                                out=acc_d[th * 1024 + t8 * 128:th * 1024 + (t8 + 1) * 128,
                                          dg * 512:(dg + 1) * 512],
                                in_=pstg[opar][:]),
                                waits=[("sDVE", d_ev)], inc="dSP", incn=16)
                            pstg_used[opar] = C["dSP"]
                        p5_used[wpar] = C["sPE"]
                proj_acc_done = C["dSP"]
            pe_prj = C["sPE"]
    
            act_prj = C["sACT"]
            dve_prj = C["sDVE"]
            dsp_prj = C["dSP"]

            with ExitStack() as mainsc:
                def sbm(name, shape, dt=F32):
                    return mainsc.enter_context(nc.sbuf_tensor(name, shape, dt))
                xet = [sbm(f"xet{i}", [128, D], BF16) for i in range(2)]
                xef = sbm("xef", [128, NDC * CAP], BF16)
                w_in = [sbm(f"w_in{i}", [128, NDC * 256], BF16) for i in range(2)]
                w4 = [sbm(f"w4_{i}", [128, 32 * 256], BF16) for i in range(2)]
                g_sb = sbm("g_sb", [128, 32 * CAP], BF16)
                osb = [sbm(f"osb{i}", [128, NST * 256], F32) for i in range(2)]
                sA = [sbm(f"sA{i}", [128, 512], F32) for i in range(2)]
                pab = [mainsc.enter_context(nc.psum_tensor(
                    f"pab{i}", [128, 512], F32)) for i in range(4)]
                po = [mainsc.enter_context(nc.psum_tensor(
                    f"po{i}", [128, 512], F32)) for i in range(2)]
                pxe = [mainsc.enter_context(nc.psum_tensor(
                    f"pxe{i}", [128, 1024], BF16)) for i in range(2)]

                ab_free_a = [act_prj, act_prj]
                ab_free_d = [dve_prj, dve_prj]
                po_free = [dve_prj, dve_prj]
                w_in_used = [pe_prj, pe_prj]
                w4_used = [pe_prj, pe_prj]
                sA_used = [dve_prj, dve_prj]
                xet_used = [pe_prj, pe_prj]
                pxe_evac = [dve_prj, dve_prj]
                xef_rdy = dve_prj
                osb_used = [dsp_prj, dsp_prj]
                arena_pe = pe_prj

                def emit_xef(e_i):
                    nonlocal xef_rdy
                    last = 0
                    for st_ in range(NST):
                        tp_ = st_ % 2
                        P.add("SP", dma(
                            xet[tp_][:],
                            xe_d[e_i * CAP + st_ * 128:e_i * CAP + (st_ + 1) * 128, :]),
                            waits=[("sPE", xet_used[tp_]), ("dWQ", scat_done)],
                            inc="dSP", incn=16)
                        xv = C["dSP"]
                        for half in range(2):
                            px = pxe[half]

                            def tpx(e, half=half, px=px, tp_=tp_):
                                ins = None
                                for i in range(8):
                                    dc = half * 8 + i
                                    ins = e.transpose(
                                        px[:, i * 128:(i + 1) * 128],
                                        xet[tp_][:, dc * 128:(dc + 1) * 128],
                                        ident_b[:])
                                return ins
                            pe_c = P.add("PE", tpx,
                                         waits=[("dSP", xv), ("sDVE", pxe_evac[half])],
                                         inc="sPE")
                            xet_used[tp_] = C["sPE"]

                            def evx(e, half=half, px=px, st_=st_):
                                dv = xef[:].rearrange("p (dc t) -> p dc t", t=CAP)
                                return e.tensor_copy(
                                    dv[:, half * 8:(half + 1) * 8,
                                       st_ * 128:(st_ + 1) * 128],
                                    px[:].rearrange("p (dc t) -> p dc t", t=128))
                            P.add("DVE", evx, waits=[("sPE", pe_c)], inc="sDVE")
                            pxe_evac[half] = C["sDVE"]
                        last = C["sDVE"]
                    xef_rdy = last

                for e_i in range(NE):
                    emit_xef(e_i)
                    xef_ok = xef_rdy
                    for j in range(32):
                        wpar = j % 2
                        P.add("POOL", dma(w_in[wpar][:], W1[e_i * 32 + j]),
                              waits=[("sPE", w_in_used[wpar])], inc="dWQ", incn=16)
                        wv = C["dWQ"]
                        for tg, (o0, n) in enumerate(((0, 512), (512, 256))):
                            ab_a = pab[2 * tg]
                            ab_b = pab[2 * tg + 1]

                            def amm(e, wpar=wpar, o0=o0, n=n, ab_a=ab_a):
                                ins = None
                                for dc in range(NDC):
                                    ins = e.matmul(
                                        ab_a[:, 0:n],
                                        w_in[wpar][:, dc * 256:dc * 256 + 128],
                                        xef[:, dc * CAP + o0:dc * CAP + o0 + n],
                                        start=(dc == 0), stop=(dc == NDC - 1))
                                return ins
                            pe_a = P.add("PE", amm,
                                         waits=[("dWQ", wv), ("sDVE", xef_ok),
                                                ("sACT", ab_free_a[tg])],
                                         inc="sPE")

                            def bmm(e, wpar=wpar, o0=o0, n=n, ab_b=ab_b):
                                ins = None
                                for dc in range(NDC):
                                    ins = e.matmul(
                                        ab_b[:, 0:n],
                                        w_in[wpar][:, dc * 256 + 128:dc * 256 + 256],
                                        xef[:, dc * CAP + o0:dc * CAP + o0 + n],
                                        start=(dc == 0), stop=(dc == NDC - 1))
                                return ins
                            pe_b2 = P.add("PE", bmm,
                                          waits=[("sDVE", ab_free_d[tg])],
                                          inc="sPE")
                            P.add("ACT", lambda e, ab_a=ab_a, tg=tg, n=n: e.activation(
                                sA[tg][:, 0:n], ab_a[:, 0:n], AF.Silu),
                                waits=[("sPE", pe_a), ("sDVE", sA_used[tg])],
                                inc="sACT")
                            ab_free_a[tg] = C["sACT"]
                            a_sil = C["sACT"]
                            P.add("DVE", lambda e, ab_b=ab_b, tg=tg, j=j, o0=o0, n=n: e.tensor_tensor(
                                g_sb[:, j * CAP + o0:j * CAP + o0 + n],
                                sA[tg][:, 0:n], ab_b[:, 0:n], ALU.mult),
                                waits=[("sPE", pe_b2), ("sACT", a_sil),
                                       ("sPE", arena_pe)],
                                inc="sDVE")
                            ab_free_d[tg] = C["sDVE"]
                            sA_used[tg] = C["sDVE"]
                        w_in_used[wpar] = C["sPE"]
                    g_done = C["sDVE"]

                    for dq in range(8):
                        wpar = dq % 2
                        P.add("POOL", dma(
                            w4[wpar][:].rearrange("p (j c) -> p j c", c=256),
                            M4[e_i * 32:(e_i + 1) * 32, :,
                               dq * 256:(dq + 1) * 256].rearrange("j p c -> p j c")),
                            waits=[("sPE", w4_used[wpar])], inc="dWQ", incn=16)
                        wv = C["dWQ"]
                        obpar = dq % 2
                        for st_ in range(NST):
                            po_par = st_ % 2

                            def omm(e, wpar=wpar, st_=st_, po_par=po_par):
                                ins = None
                                for j in range(32):
                                    ins = e.matmul(
                                        po[po_par][:, 0:256],
                                        g_sb[:, j * CAP + st_ * 128:j * CAP + (st_ + 1) * 128],
                                        w4[wpar][:, j * 256:(j + 1) * 256],
                                        start=(j == 0), stop=(j == 31))
                                return ins
                            pe_o = P.add("PE", omm,
                                         waits=[("dWQ", wv), ("sDVE", g_done),
                                                ("sDVE", po_free[po_par])],
                                         inc="sPE")
                            P.add("DVE", lambda e, st_=st_, obpar=obpar, po_par=po_par: e.tensor_copy(
                                osb[obpar][:, st_ * 256:(st_ + 1) * 256],
                                po[po_par][:, 0:256]),
                                waits=[("sPE", pe_o), ("dSP", osb_used[obpar])],
                                inc="sDVE")
                            po_free[po_par] = C["sDVE"]
                        o_ev = C["sDVE"]
                        for st_ in range(NST):
                            P.add("SP", lambda e, e_i=e_i, dq=dq, st_=st_, obpar=obpar: e.dma_start(
                                out=o_d[e_i * CAP + st_ * 128:e_i * CAP + (st_ + 1) * 128,
                                        dq * 256:(dq + 1) * 256],
                                in_=osb[obpar][:, st_ * 256:(st_ + 1) * 256]),
                                waits=[("sDVE", o_ev)], inc="dSP", incn=16)
                        osb_used[obpar] = C["dSP"]
                        w4_used[wpar] = C["sPE"]
                    arena_pe = C["sPE"]
                o_writes = C["dSP"]
                adds_done = o_writes

            with ExitStack() as epi:
                ep = [epi.enter_context(nc.sbuf_tensor(
                    f"ep{i}", [128, D], F32)) for i in range(2)]
                xo2 = [epi.enter_context(nc.sbuf_tensor(
                    f"xo2_{i}", [128, D], F32)) for i in range(2)]
                stg = [epi.enter_context(nc.sbuf_tensor(
                    f"stg{i}", [128, D], F32)) for i in range(2)]
                bias_bc = epi.enter_context(nc.sbuf_tensor(
                    "bias_bc", [128, D], F32))
                pbe = epi.enter_context(nc.psum_tensor("pbe", [128, 512], F32))
                for half in range(4):
                    mmb = P.add("PE", lambda e, half=half: e.matmul(
                        pbe[:, 0:512], onesr[:],
                        brow[0:1, half * 512:(half + 1) * 512],
                        start=True, stop=True),
                        waits=[("dSP", idx_consts), ("sDVE", ones_done),
                               ("sDVE", C["sDVE"]), ("sPE", arena_pe)], inc="sPE")
                    P.add("DVE", lambda e, half=half: e.tensor_copy(
                        bias_bc[:, half * 512:(half + 1) * 512], pbe[:, 0:512]),
                        waits=[("sPE", mmb)], inc="sDVE")
                bias_bc_done = C["sDVE"]
                ep_used = [0, 0]
                xo_used = [0, 0]
                stg_used = [0, 0]
                ost_dma = [0, 0]
                for tt in range(16):
                    par = tt % 2
                    P.add("SP", dma(ep[par][:],
                                    acc_d[tt * 128:(tt + 1) * 128, :]),
                          waits=[("dSP", adds_done), ("sDVE", ep_used[par]),
                                 ("dSP", proj_acc_done)],
                          inc="dSP", incn=16)
                    av = C["dSP"]
                    P.add("SP", dma(xo2[par][:],
                                    xown[tt * 128:(tt + 1) * 128, :]),
                          waits=[("sDVE", xo_used[par])], inc="dSP", incn=16)
                    xv = C["dSP"]
                    P.add("POOL", lambda e, tt=tt, par=par: e.indirect_dma_start(
                        out=stg[par][:], out_offset=None,
                        in_=o_d,
                        in_offset=IndirectOffsetOnAxis(d1i[:, tt:tt + 1], 0),
                        bounds_check=None, oob_is_err=False),
                        waits=[("dSP", o_writes), ("sDVE", stg_used[par])],
                        inc="dWQ", incn=16)
                    g1v = C["dWQ"]
                    P.add("DVE", lambda e, tt=tt, par=par: e.scalar_tensor_tensor(
                        ep[par][:], stg[par][:], g1[:, tt:tt + 1], ep[par][:],
                        ALU.mult, ALU.add),
                        waits=[("dWQ", g1v), ("dSP", av), ("dOUT", ost_dma[par])],
                        inc="sDVE")
                    d_a1 = C["sDVE"]
                    P.add("POOL", lambda e, tt=tt, par=par: e.indirect_dma_start(
                        out=stg[par][:], out_offset=None,
                        in_=o_d,
                        in_offset=IndirectOffsetOnAxis(d2i[:, tt:tt + 1], 0),
                        bounds_check=None, oob_is_err=False),
                        waits=[("sDVE", d_a1)], inc="dWQ", incn=16)
                    g2v = C["dWQ"]
                    P.add("DVE", lambda e, tt=tt, par=par: e.scalar_tensor_tensor(
                        ep[par][:], stg[par][:], g2[:, tt:tt + 1], ep[par][:],
                        ALU.mult, ALU.add),
                        waits=[("dWQ", g2v)], inc="sDVE")
                    stg_used[par] = C["sDVE"]
                    P.add("DVE", lambda e, par=par: e.tensor_tensor(
                        ep[par][:], ep[par][:], xo2[par][:], ALU.add),
                        waits=[("dSP", xv)], inc="sDVE")
                    xo_used[par] = C["sDVE"]
                    P.add("DVE", lambda e, par=par: e.tensor_tensor(
                        ep[par][:], ep[par][:], bias_bc[:], ALU.add),
                        waits=[("sDVE", bias_bc_done)], inc="sDVE")
                    ep_used[par] = C["sDVE"]
                    d_fin = C["sDVE"]
                    P.add("SP", dma(y[tt * 128:(tt + 1) * 128, :], ep[par][:]),
                          waits=[("sDVE", d_fin)], inc="dOUT", incn=16)
                    ost_dma[par] = C["dOUT"]

            P.add("SP", lambda e: e.nop(), waits=[("dOUT", C["dOUT"])])

        block = top.enter_context(nc.Block())
        P.emit(block, sems)

    return nc


_NC_CACHE = {}


def _get_nc():
    if "nc" not in _NC_CACHE:
        _NC_CACHE["nc"] = build_program()
    return _NC_CACHE["nc"]


def _host_layout(caw, cab, cpw, cpb, rw, win, wout, fb):
    A1h = np.ascontiguousarray(
        caw[:D].reshape(NHEADS, 128, NDC, 128).transpose(0, 3, 2, 1)
    ).reshape(NHEADS, 128, NDC * 128)
    kv_t = caw[D:].reshape(2, 128, NDC, 128)
    KVi = np.empty((128, NDC * 256), np.float32)
    for dc in range(NDC):
        KVi[:, dc * 256:dc * 256 + 128] = kv_t[0, :, dc, :].T
        KVi[:, dc * 256 + 128:dc * 256 + 256] = kv_t[1, :, dc, :].T
    R1h = np.ascontiguousarray(
        rw.reshape(NE, NDC, 128).transpose(2, 1, 0)).reshape(128, NDC * NE)
    P5h = np.ascontiguousarray(cpw.T).reshape(NDC, 128, 2048)
    W1h = np.ascontiguousarray(
        win.reshape(NE, 2, 32, 128, NDC, 128).transpose(0, 2, 5, 4, 1, 3)
    ).reshape(NE * 32, 128, NDC * 256)
    M4h = np.ascontiguousarray(wout.transpose(0, 2, 1)).reshape(NE * 32, 128, 2048)
    bqh_all = cab[:D].reshape(NHEADS, 128)
    bkh = np.ascontiguousarray(cab[D:D + 128].reshape(128, 1))
    bvh = np.ascontiguousarray(cab[D + 128:].reshape(1, 128))
    bsh = np.ascontiguousarray((cpb + fb).reshape(NDC, 128).T)
    bsrh = np.ascontiguousarray((cpb + fb).reshape(1, 2048))
    trih = np.triu(np.ones((128, 128), np.float32), 1)
    ioth = np.ascontiguousarray(
        np.arange(2048, dtype=np.float32).reshape(16, 128).T)
    io0h = np.arange(128, dtype=np.int32).reshape(128, 1)
    kp = np.arange(128, dtype=np.int64)[:, None]
    tok = np.arange(512, dtype=np.int64)[None, :]
    mskh = np.concatenate(
        [(r * 128 + kp <= tok).astype(np.float32) for r in range(4)], axis=1)
    idnh = np.eye(128, dtype=np.float32)
    return (A1h, KVi, R1h, P5h, W1h, M4h, bqh_all, bkh, bvh, bsh, mskh, idnh,
            bsrh, trih, ioth, io0h)


def kernel(L0_attn_hidden_states, L0_attn_c_attn_w, L0_attn_c_attn_b,
           L0_attn_c_proj_w, L0_attn_c_proj_b, L0_ffn_router_weight,
           L0_ffn_w_in, L0_ffn_w_out, L0_ffn_bias):
    x = np.ascontiguousarray(np.asarray(L0_attn_hidden_states, np.float32))
    caw = np.asarray(L0_attn_c_attn_w, np.float32)
    cab = np.asarray(L0_attn_c_attn_b, np.float32)
    cpw = np.asarray(L0_attn_c_proj_w, np.float32)
    cpb = np.asarray(L0_attn_c_proj_b, np.float32)
    rw = np.asarray(L0_ffn_router_weight, np.float32)
    win = np.asarray(L0_ffn_w_in, np.float32)
    wout = np.asarray(L0_ffn_w_out, np.float32)
    fb = np.asarray(L0_ffn_bias, np.float32)

    (A1h, KVi, R1h, P5h, W1h, M4h, bqh_all, bkh, bvh, bsh, mskh, idnh,
     bsrh, trih, ioth, io0h) = _host_layout(caw, cab, cpw, cpb, rw, win, wout, fb)

    in_maps = []
    for c in range(8):
        b, h = c // 2, c % 2
        xb = x[b]
        xo = np.ascontiguousarray(xb[h * TOWN:(h + 1) * TOWN])
        bqh = np.ascontiguousarray(bqh_all[h * NH:(h + 1) * NH].T)
        hmv = np.zeros((128, 2), np.float32)
        hmv[:, 0] = 1.0 - h
        hmv[:, 1] = float(h)
        in_maps.append({
            "xin": xb, "xown": xo,
            "A1": A1h[h * NH:(h + 1) * NH], "KV1": KVi, "R1": R1h,
            "P5": P5h, "W1": W1h, "M4": M4h,
            "bqd": bqh, "bkd": bkh, "bvd": bvh, "bsd": bsh,
            "mskd": mskh, "idnd": idnh, "hmd": hmv,
            "bsrd": bsrh, "trid": trih, "iotd": ioth, "io0d": io0h,
        })

    nc = _get_nc()
    res = run_bass_kernel_spmd(nc, in_maps, core_ids=list(range(8)))

    out = np.empty((NB, S, D), np.float32)
    for c in range(8):
        b, h = c // 2, c % 2
        out[b, h * TOWN:(h + 1) * TOWN, :] = res.results[c]["y"]
    return out


if __name__ == "__main__":
    import time
    t0 = time.time()
    nc = build_program()
    t1 = time.time()
    n = sum(1 for _ in nc.all_instructions())
    print(f"build {t1 - t0:.1f}s, {n} instructions")
